# version 06 (final)
# speedup vs baseline: 1.2877x; 1.0671x over previous
import numpy as np
import ml_dtypes
from contextlib import ExitStack

import concourse.bass as bass
import concourse.mybir as mybir
from concourse.bass_utils import run_bass_kernel_spmd

F32 = mybir.dt.float32
BF16 = mybir.dt.bfloat16
AF = mybir.ActivationFunctionType
OP = mybir.AluOpType
AX = mybir.AxisListType
BF = ml_dtypes.bfloat16

NCORES = 8
R = 2048
S = 4096
D = 1024
RETNET_DEBUG = [False]
QB = 256
NQ = R // QB
NK = S // 128
LN_EPS = 1e-5

GELU_A = float(np.sqrt(2.0 / np.pi) * 2.0 * 0.044715)
GELU_B = float(1.0 / 0.044715)


class Prog:
    ENGINES = ("tensor", "vector", "scalar", "sync", "gpsimd")

    def __init__(self, nc):
        self.nc = nc
        self.es = ExitStack()
        self.q = {e: [] for e in self.ENGINES}
        self.tot = {}
        self.paranoid = False

    def sem(self, name):
        h = self.es.enter_context(self.nc.semaphore(name))
        self.tot[h] = 0
        return h

    def op(self, eng, fn, waits=(), inc=None, inc_n=1):
        if self.paranoid:
            waits = list(waits) + [(s, self.tot[s]) for s in self.tot]
        if inc is not None:
            self.tot[inc] += inc_n
        tot = self.tot[inc] if inc is not None else None
        waits = tuple((s, v) for (s, v) in waits if v and v > 0)

        def run2(e, fn=fn, waits=waits, inc=inc, inc_n=inc_n):
            for (s, v) in waits:
                e.wait_ge(s, v)
            ins = fn(e)
            if inc is not None:
                ins.then_inc(inc, inc_n)
        self.q[eng].append(run2)
        return tot

    def now(self, sem):
        return self.tot[sem]

    def finalize(self):
        with self.nc.Block() as block:
            for e in self.ENGINES:
                q = self.q[e]
                if q:
                    getattr(block, e)(lambda eng, q=q: [f(eng) for f in q])


class NS:
    pass



def prog_barrier(P, B):
    waits = [(s, P.now(s)) for s in P.tot]
    for eng in ("tensor", "vector", "scalar", "sync", "gpsimd"):
        P.op(eng, lambda e: e.nop(), waits=waits)


def alloc_shared(nc, B):
    B.KT = nc.alloc_sbuf_tensor("KT", [128, 8 * S], BF16)
    B.V_aug = nc.alloc_sbuf_tensor("V_aug", [128, NK * VSTRIDE], BF16)
    B.xg = nc.alloc_sbuf_tensor("xg", [128, D], F32)
    B.Wb = nc.alloc_sbuf_tensor("Wb", [128, 8 * D], BF16)
    B.scr_ln = nc.alloc_sbuf_tensor("scr_ln", [128, D], F32)
    B.scr_b = nc.alloc_sbuf_tensor("scr_b", [128, D], F32)
    B.scr_c = nc.alloc_sbuf_tensor("scr_c", [128, 512], F32)
    B.scr_a = B.scr_ln
    B.stats = nc.alloc_sbuf_tensor("stats", [128, 16], F32)
    B.xn_bf = nc.alloc_sbuf_tensor("xn_bf", [128, D], BF16)
    B.scr_sq = B.xn_bf
    B.XTg = nc.alloc_sbuf_tensor("XTg", [128, 8 * 512], BF16)
    B.brow = nc.alloc_sbuf_tensor("brow", [1, D], BF16)
    B.tabs = nc.alloc_sbuf_tensor("tabs", [128, 512], F32)
    B.tab_c = B.tabs[:, 0:256]
    B.tab_s = B.tabs[:, 256:512]
    B.greps = nc.alloc_sbuf_tensor("greps", [128, 2 * D], F32)
    B.hreps = B.scr_c[:, 256:512]
    B.st16 = None
    B.b1col = nc.alloc_sbuf_tensor("b1col", [128, NH], F32)
    B.comb_sb = nc.alloc_sbuf_tensor("comb_sb", [128, 16 * 16], F32)
    B.nsc = nc.alloc_sbuf_tensor("nsc", [1, 8], F32)
    B.nsc_bf = nc.alloc_sbuf_tensor("nsc_bf", [1, 2], BF16)
    B.negm = nc.alloc_sbuf_tensor("negm", [128, 1], F32)
    B.E_sb = nc.alloc_sbuf_tensor("E_sb", [128, 2 * QB], BF16)
    B.ctx64_full = nc.alloc_sbuf_tensor("ctx64", [128, NH * QB], BF16)
    B.ctx64 = B.ctx64_full[0:64, :]
    B.ctx_pair = nc.alloc_sbuf_tensor("ctx_pair", [128, 8 * QB], BF16)
    B.rep_sb = nc.alloc_sbuf_tensor("rep_sb", [64, QB], F32)
    B.recd64 = nc.alloc_sbuf_tensor("recd64", [128, QB], F32)
    B.attn_sb = nc.alloc_sbuf_tensor("attn_sb", [128, D], F32)
    B.xrow = nc.alloc_sbuf_tensor("xrow", [128, D], F32)
    B.gw16 = nc.alloc_sbuf_tensor("gw16", [128, 8 * 16], BF16)
    B.gb_bf = nc.alloc_sbuf_tensor("gb_bf", [1, 16], BF16)
    B.gb_rep = nc.alloc_sbuf_tensor("gb_rep", [128, 16], F32)
    greps_bf = B.greps.bitcast(BF16)
    B.borep = B.greps[:, 1024:2048]
    B.qstage = B.tabs.bitcast(BF16)
    B.act_bf = greps_bf[:, 2048:3072]
    B.st16 = B.comb_sb[:, 0:96]
    B.z_sb = nc.alloc_sbuf_tensor("z_sb", [128, 8], F32)
    B.z_bf = nc.alloc_sbuf_tensor("z_bf", [128, 8], BF16)
    B.krow = greps_bf[:, 0:1024]
    B.kd_bf = greps_bf[:, 1024:2048]
    B.A_T = B.recd64[:, 0:128]
    B.msk = B.recd64[:, 128:256]
    B.mcl = B.E_sb.bitcast(F32)[:, 0:128]
    B.pp_bf = B.E_sb[:, 256:384]
    B.cross_scr = B.attn_sb
    B.scr2 = B.ctx_pair.bitcast(F32)
    B.gcols = nc.alloc_sbuf_tensor("gcols", [128, 8], F32)
    B.grow = nc.alloc_sbuf_tensor("grow", [1, 8], F32)
    B.growp = B.comb_sb[0:1, 0:256]
    B.g_bf = nc.alloc_sbuf_tensor("g_bf", [1, 8], BF16)
    B.S_sb = B.KT.bitcast(F32)
    B.S_bf = B.V_aug[:, 0:8 * D]
    B.QT_g = B.V_aug[:, 8 * D:8 * D + 8 * 512]
    B.KT_g = B.V_aug[:, 8 * D + 8 * 512:8 * D + 16 * 512]
    B.V_g = B.V_aug[:, 8 * D + 16 * 512:8 * D + 16 * 512 + 4 * D]
    B.w13_raw = B.KT[:, 0:8 * D]
    B.w13T = B.KT[:, 8 * D:16 * D]
    B.w2_raw = B.KT[:, 16 * D:16 * D + 8 * 512]
    B.w2T = B.KT[:, 16 * D + 8 * 512:16 * D + 8 * 512 + 4 * D]
    B.XT_m16 = B.V_aug[:, 0:8 * R]


def mkprog():
    nc = bass.Bass("TRN2", target_bir_lowering=False, debug=False, dynamic_dma_scratch_size=16384)
    P = Prog(nc)
    B = NS()
    for s in ("sLD", "sGD", "sMM", "sV", "sA", "sCC"):
        setattr(B, s, P.sem(s))
    return nc, P, B


HOST_CONSTS = {
    "identity": np.eye(128, dtype=BF),
    "identity_f32": np.eye(128, dtype=np.float32),
    "ones_bf": np.ones((128, 128), dtype=BF),
    "ones_f32": np.ones((128, 128), dtype=np.float32),
}


def declare_consts(nc, P, B, idf32=False, onesf32=False):
    B.identity = nc.alloc_sbuf_tensor("c_identity", [128, 128], BF16)
    B.ones_bf = nc.alloc_sbuf_tensor("c_ones_bf", [128, 128], BF16)
    d_id = nc.dram_tensor("identity", [128, 128], BF16, kind="ExternalInput").ap()
    d_ob = nc.dram_tensor("ones_bf", [128, 128], BF16, kind="ExternalInput").ap()
    P.op("sync", lambda e: e.dma_start(out=B.identity[:], in_=d_id[:, :]), inc=B.sLD, inc_n=16)
    P.op("sync", lambda e: e.dma_start(out=B.ones_bf[:], in_=d_ob[:, :]), inc=B.sLD, inc_n=16)
    if idf32:
        B.identity_f32 = nc.alloc_sbuf_tensor("c_identity_f32", [128, 128], F32)
        d_if = nc.dram_tensor("identity_f32", [128, 128], F32, kind="ExternalInput").ap()
        P.op("sync", lambda e: e.dma_start(out=B.identity_f32[:], in_=d_if[:, :]), inc=B.sLD, inc_n=16)
    if onesf32:
        B.ones_f32 = nc.alloc_sbuf_tensor("c_ones_f32", [128, 128], F32)
        d_of = nc.dram_tensor("ones_f32", [128, 128], F32, kind="ExternalInput").ap()
        P.op("sync", lambda e: e.dma_start(out=B.ones_f32[:], in_=d_of[:, :]), inc=B.sLD, inc_n=16)
    else:
        B.crow = nc.alloc_sbuf_tensor("c_crow", [128, 64], F32)
        P.op("vector", lambda e: e.memset(B.crow[:, :], 1.0), inc=B.sV)


def emit_ln(P, B, src_ap, dst_bf, w_src, eps=LN_EPS, gain_rep=None, bias_rep=None,
            dst2_f32=None):
    st = B.stats
    entryA = P.now(B.sA)
    entryV = P.now(B.sV)
    entryM = P.now(B.sMM)
    a1 = P.op("scalar", lambda e: e.activation(B.scr_sq[:, 0:D], src_ap, AF.Square,
                                               accum_out=st[:, 0:1]),
              waits=list(w_src) + [(B.sV, entryV), (B.sMM, entryM)], inc=B.sA)
    v1 = P.op("vector", lambda e: e.reduce_sum(st[:, 1:2], src_ap, axis=AX.X),
              waits=w_src, inc=B.sV)
    v2 = P.op("vector", lambda e: e.tensor_scalar(st[:, 2:3], st[:, 1:2], 1.0 / D, None, OP.mult),
              waits=[(B.sV, v1)], inc=B.sV)
    v3 = P.op("vector", lambda e: e.tensor_tensor(st[:, 3:4], st[:, 2:3], st[:, 2:3], OP.mult),
              waits=[(B.sV, v2)], inc=B.sV)
    v4 = P.op("vector", lambda e: e.scalar_tensor_tensor(st[:, 4:5], st[:, 0:1], 1.0 / D,
                                                         st[:, 3:4], OP.mult, OP.subtract),
              waits=[(B.sA, a1), (B.sV, v3)], inc=B.sV)
    v5 = P.op("vector", lambda e: e.tensor_scalar(st[:, 5:6], st[:, 4:5], eps, None, OP.add),
              waits=[(B.sV, v4)], inc=B.sV)
    v6 = P.op("vector", lambda e: e.reciprocal(st[:, 6:7], st[:, 5:6]),
              waits=[(B.sV, v5), (B.sA, entryA)], inc=B.sV)
    a2 = P.op("scalar", lambda e: e.activation(st[:, 6:7], st[:, 6:7], AF.Sqrt),
              waits=[(B.sV, v6)], inc=B.sA)
    v7 = P.op("vector", lambda e: e.tensor_scalar(st[:, 7:8], st[:, 2:3], -1.0, None, OP.mult),
              waits=[(B.sV, v3), (B.sA, entryA)], inc=B.sV)
    v8 = P.op("vector", lambda e: e.tensor_tensor(st[:, 7:8], st[:, 7:8], st[:, 6:7], OP.mult),
              waits=[(B.sV, v7), (B.sA, a2)], inc=B.sV)
    if gain_rep is None:
        a3 = P.op("vector", lambda e: e.tensor_scalar(dst_bf, src_ap, st[:, 6:7],
                                                      st[:, 7:8], OP.mult, OP.add),
                  waits=[(B.sV, v8), (B.sMM, entryM)], inc=B.sV)
        vL = a3
        if dst2_f32 is not None:
            vL = P.op("vector", lambda e: e.tensor_scalar(dst2_f32, src_ap, st[:, 6:7],
                                                          st[:, 7:8], OP.mult, OP.add),
                      waits=[(B.sV, v8)], inc=B.sV)
        a3 = vL
        return (B.sV, a3), vL
    else:
        a3a = P.op("vector", lambda e: e.tensor_scalar(B.scr_ln[:, 0:D], src_ap, st[:, 6:7],
                                                       st[:, 7:8], OP.mult, OP.add),
                   waits=[(B.sV, v8), (B.sMM, entryM)], inc=B.sV)
        v9 = P.op("vector", lambda e: e.tensor_tensor(B.scr_ln[:, 0:D], B.scr_ln[:, 0:D],
                                                      gain_rep, OP.mult),
                  waits=[(B.sV, a3a)], inc=B.sV)
        v10 = P.op("vector", lambda e: e.tensor_tensor(dst_bf, B.scr_ln[:, 0:D],
                                                       bias_rep, OP.add),
                   waits=[(B.sV, v9)], inc=B.sV)
        vL = v10
        if dst2_f32 is not None:
            vL = P.op("vector", lambda e: e.tensor_copy(dst2_f32, dst_bf),
                      waits=[(B.sV, v10)], inc=B.sV)
        return (B.sV, vL), vL


def emit_transpose(P, B, src_bf, dst_sb, ps_slot, waits, evac_waits=()):
    m = P.op("tensor", lambda e: e.transpose(ps_slot, src_bf, B.identity[:]),
             waits=list(waits) + [(B.sV, P.now(B.sV))], inc=B.sMM)
    v = P.op("vector", lambda e: e.tensor_copy(dst_sb, ps_slot),
             waits=[(B.sMM, m)] + list(evac_waits), inc=B.sV)
    return v


def emit_replicate_row(P, B, src_row_bf, dst, ps, n, waits=()):
    m = P.op("tensor", lambda e: e.matmul(ps, B.ones_bf[0:1, :], src_row_bf,
                                          start=True, stop=True),
             waits=waits, inc=B.sMM)
    v = P.op("vector", lambda e: e.tensor_copy(dst, ps), waits=[(B.sMM, m)], inc=B.sV)
    return v


def alloc_psum(nc):
    ps = NS()
    ps.A = nc.alloc_psum_tensor("psA", [128, 512], F32)
    ps.B = nc.alloc_psum_tensor("psB", [128, 512], F32)
    ps.T = nc.alloc_psum_tensor("psT", [128, 512], F32)
    ps.N = nc.alloc_psum_tensor("psN", [128, 512], F32)
    ps.C = [nc.alloc_psum_tensor(f"psC{i}", [128, 512], F32) for i in range(4)]
    return ps


def tr_slot(ps, i):
    return ps.T.bitcast(BF16)[:, (i % 4) * 128:((i % 4) + 1) * 128]


def emit_L1_attn_b(nc, P, B, ps, io):
    pref = io["pref"]
    x_own = io["x_own"]
    x_ctx = io["x_ctx"]
    x_out = io["x_out"]
    qt_dram = io["qt_dram"]
    wts = {n: nc.dram_tensor(pref + n, [D, D], F32, kind="ExternalInput").ap()
           for n in ("wqT", "wkT", "wvT", "woT")}
    bias = {n: nc.dram_tensor(pref + n, [1, D], F32, kind="ExternalInput").ap()
            for n in ("b_q", "b_k", "b_v", "b_out")}
    KT = B.KT
    V = B.V_aug[:, 0:NK * D]
    xg = B.xg
    xn_bf = B.xn_bf
    XT = B.XTg
    Wb = B.Wb
    bcol = B.b1col[:, 0:8]
    brow_o = B.borep
    bo_bf = B.brow
    sqn = B.E_sb[:, 0:512]
    nsc = B.nsc
    nsc_bf = B.nsc_bf
    negm = B.negm
    QTb = B.ctx64_full
    E_sb = B.E_sb
    ctx_sb = B.ctx_pair
    recd = B.recd64[0:1, 0:QB]
    recdT = B.rep_sb[0:128, 0:2] if False else B.recd64[:, 4:6]
    attn_sb = B.attn_sb
    xrow = B.xrow
    xg_slots = [B.xg, B.scr_b]
    xg_guard = [(), ()]
    P.op("vector", lambda e: e.memset(nsc[0:1, 0:4], 0.0),
         waits=[(B.sA, P.now(B.sA)), (B.sMM, P.now(B.sMM))], inc=B.sV)

    def production(src_ap, n_rows, wname, bname, dst):
        wld = P.op("gpsimd", lambda e: e.dma_start(
            out=Wb.rearrange("p (i n) -> p i n", i=8),
            in_=wts[wname].rearrange("(i p) n -> p i n", p=128)),
            waits=[(B.sMM, P.now(B.sMM))], inc=B.sGD, inc_n=16)
        bld = P.op("gpsimd", lambda e: e.dma_start(
            out=bcol.rearrange("p (o j) -> p o j", o=1),
            in_=bias[bname].rearrange("o (j p) -> p o j", p=128),
            allow_slow_non_contiguous=True),
            waits=[(B.sA, P.now(B.sA))], inc=B.sGD, inc_n=16)
        nrm_col = {"QT": 0, "KT": 1, "V": None}[dst]
        n_groups = n_rows // 512
        qtdma_prev = [0]
        for g in range(n_groups):
            xt_v = 0
            lastA = 0
            for t in range(4):
                sl = t % 2
                ld = P.op("sync", lambda e, g=g, t=t, sl=sl: e.dma_start(
                    out=xg_slots[sl][:, 0:D],
                    in_=src_ap[g * 512 + t * 128:g * 512 + (t + 1) * 128, :]),
                    waits=xg_guard[sl], inc=B.sLD, inc_n=16)
                aw, vL = emit_ln(P, B, xg_slots[sl][:, 0:D],
                                 xn_bf[:, 0:D], [(B.sLD, ld)])
                xg_guard[sl] = [aw, (B.sV, vL)]
                for i in range(8):
                    xt_v = emit_transpose(P, B, xn_bf[:, i * 128:(i + 1) * 128],
                                          XT[:, i * 512 + t * 128:i * 512 + (t + 1) * 128],
                                          tr_slot(ps, t * 8 + i),
                                          waits=[aw, (B.sV, P.now(B.sV))])
            if dst in ("KT", "QT"):
                for j in range(8):
                    bank = [ps.A, ps.B][j % 2]
                    mf = None
                    for i in range(8):
                        w = [(B.sV, xt_v), (B.sGD, max(wld, bld)),
                             (B.sV, P.now(B.sV)), (B.sA, P.now(B.sA))] if i == 0 else []
                        mf = P.op("tensor", lambda e, i=i, j=j, bank=bank: e.matmul(
                            bank[:, 0:512],
                            Wb[:, i * D + j * 128:i * D + (j + 1) * 128],
                            XT[:, i * 512:(i + 1) * 512],
                            start=(i == 0), stop=(i == 7)),
                            waits=w, inc=B.sMM)
                    if dst == "KT":
                        out_sl = KT[:, j * S + g * 512: j * S + (g + 1) * 512]
                    else:
                        out_sl = None
                    tgt = (KT[:, j * S + g * 512: j * S + (g + 1) * 512] if dst == "KT"
                           else QTb[:, j * 512:(j + 1) * 512])
                    ev_w = [(B.sMM, mf), (B.sGD, bld)]
                    if dst == "QT":
                        ev_w.append((B.sLD, qtdma_prev[0]))
                    ev = P.op("vector", lambda e, tgt=tgt, j=j, bank=bank: e.tensor_scalar(
                        tgt, bank[:, 0:512], bcol[:, j:j + 1], None, OP.add),
                        waits=ev_w, inc=B.sV)
                    sq = P.op("vector", lambda e, tgt=tgt: e.tensor_tensor(
                        sqn[:, 0:512], tgt, tgt, OP.mult),
                        waits=[(B.sV, ev)], inc=B.sV)
                    P.op("tensor", lambda e, j=j: e.matmul(
                        ps.N[0:1, 0:512], B.ones_bf[:, 0:1], sqn[:, 0:512],
                        start=(j == 0), stop=(j == 7)),
                        waits=[(B.sV, sq)], inc=B.sMM)
                nrm_mm = P.now(B.sMM)
                rv_ = P.op("vector", lambda e: e.reduce_max(nsc[0:1, 2:3], ps.N[0:1, 0:512], axis=AX.X),
                    waits=[(B.sMM, nrm_mm)], inc=B.sV)
                P.op("vector", lambda e, nrm_col=nrm_col: e.tensor_tensor(
                    nsc[0:1, nrm_col:nrm_col + 1], nsc[0:1, nrm_col:nrm_col + 1],
                    nsc[0:1, 2:3], OP.max),
                    waits=[(B.sV, rv_)], inc=B.sV)
                if dst == "QT":
                    qtdma_prev[0] = P.op("sync", lambda e, g=g: e.dma_start(
                        out=qt_dram.rearrange("p (i r) -> p i r", i=8)[:, :, g * 512:(g + 1) * 512],
                        in_=QTb.rearrange("p (i r) -> p i r", i=8)[:, :, 0:512]),
                        waits=[(B.sV, P.now(B.sV)), (B.sA, P.now(B.sA))], inc=B.sLD, inc_n=16)
            else:
                vld = P.op("gpsimd", lambda e: e.dma_start(out=bo_bf[0:1, 0:D], in_=bias["b_v"][:, :]),
                           inc=B.sGD, inc_n=16) if g == 0 else None
                for rt in range(4):
                    bank = [ps.A, ps.B][rt % 2]
                    for ncol in range(2):
                        mf = None
                        for i in range(8):
                            w = [(B.sV, xt_v), (B.sGD, P.now(B.sGD)),
                                 (B.sA, P.now(B.sA))] if (i == 0) else []
                            mf = P.op("tensor", lambda e, i=i, rt=rt, ncol=ncol, bank=bank: e.matmul(
                                bank[:, 0:512],
                                XT[:, i * 512 + rt * 128:i * 512 + (rt + 1) * 128],
                                Wb[:, i * D + ncol * 512:i * D + (ncol + 1) * 512],
                                start=(i == 0), stop=False),
                                waits=w, inc=B.sMM)
                        mf = P.op("tensor", lambda e, ncol=ncol, bank=bank: e.matmul(
                            bank[:, 0:512], B.ones_bf[0:1, :],
                            bo_bf[0:1, ncol * 512:(ncol + 1) * 512],
                            start=False, stop=True), inc=B.sMM)
                        kt_idx = g * 4 + rt
                        P.op("scalar", lambda e, kt_idx=kt_idx, ncol=ncol, bank=bank: e.activation(
                            V[:, kt_idx * D + ncol * 512: kt_idx * D + (ncol + 1) * 512],
                            bank[:, 0:512], AF.Copy),
                            waits=[(B.sMM, mf)], inc=B.sA)

    production(x_ctx, S, "wkT", "b_k", "KT")
    production(x_ctx, S, "wvT", "b_v", "V")
    production(x_own, R, "wqT", "b_q", "QT")

    vm = P.op("vector", lambda e: e.tensor_tensor(nsc[0:1, 2:3], nsc[0:1, 0:1],
                                                  nsc[0:1, 1:2], OP.mult),
              waits=[(B.sV, P.now(B.sV))], inc=B.sV)
    am = P.op("scalar", lambda e: e.activation(nsc[0:1, 3:4], nsc[0:1, 2:3], AF.Sqrt),
              waits=[(B.sV, vm)], inc=B.sA)
    vneg = P.op("vector", lambda e: e.tensor_scalar(nsc_bf[0:1, 0:1], nsc[0:1, 3:4],
                                                    -1.0 / 32.0, None, OP.mult),
                waits=[(B.sA, am)], inc=B.sV)
    mrep = P.op("tensor", lambda e: e.matmul(ps.N[:, 0:1], B.ones_bf[0:1, :],
                                             nsc_bf[0:1, 0:1], start=True, stop=True),
                waits=[(B.sV, vneg)], inc=B.sMM)
    vnegm = P.op("vector", lambda e: e.tensor_copy(negm[:, 0:1], ps.N[:, 0:1]),
                 waits=[(B.sMM, mrep)], inc=B.sV)

    bo_ld = P.op("gpsimd", lambda e: e.dma_start(out=bo_bf[0:1, 0:D], in_=bias["b_out"][:, :]),
                 waits=[(B.sMM, P.now(B.sMM))], inc=B.sGD, inc_n=16)
    for h in range(2):
        emit_replicate_row(P, B, bo_bf[0:1, h * 512:(h + 1) * 512],
                           brow_o[:, h * 512:(h + 1) * 512], ps.T[:, 0:512],
                           512, waits=[(B.sGD, bo_ld), (B.sV, P.now(B.sV))])

    wo_ld = P.op("gpsimd", lambda e: e.dma_start(
        out=Wb.rearrange("p (i n) -> p i n", i=8),
        in_=wts["woT"].rearrange("(i p) n -> p i n", p=128)),
        waits=[(B.sMM, P.now(B.sMM))], inc=B.sGD, inc_n=16)

    exp_tot = {}
    av_tot = {}
    ctx_evac_v = 0
    den_recip_v = 0
    outp_evac_a = 0
    st_tot = 0
    for qb in range(NQ):
        qp = qb % 2
        qld = P.op("sync", lambda e, qb=qb, qp=qp: e.dma_start(
            out=QTb[:, qp * 8 * QB:(qp + 1) * 8 * QB],
            in_=qt_dram.rearrange("p (i r) -> p i r", i=8)[:, :, qb * QB:(qb + 1) * QB]),
            waits=[(B.sMM, P.now(B.sMM))], inc=B.sLD, inc_n=16)
        for kt in range(NK):
            pr = (kt // 2) % 2
            bank = [ps.A, ps.B][pr]
            half = kt % 2
            for i in range(8):
                w = []
                if i == 0:
                    w = [(B.sLD, qld), (B.sV, vnegm)]
                    key = (qb, kt - 2) if kt >= 2 else (qb - 1, NK + kt - 2)
                    w.append((B.sA, exp_tot.get(key, 0)))
                    if kt == 0:
                        w.append((B.sV, ctx_evac_v))
                        w.append((B.sV, den_recip_v))
                P.op("tensor", lambda e, i=i, kt=kt, qb=qb, qp=qp, bank=bank, half=half: e.matmul(
                    bank[:, half * QB:(half + 1) * QB],
                    KT[:, i * S + kt * 128:i * S + (kt + 1) * 128],
                    QTb[:, qp * 8 * QB + i * QB:qp * 8 * QB + (i + 1) * QB],
                    start=(i == 0), stop=(i == 7)),
                    waits=w, inc=B.sMM)
            if half == 1:
                exp_tot[(qb, kt - 1)] = exp_tot[(qb, kt)] = P.op(
                    "scalar", lambda e, bank=bank: e.activation(
                        E_sb[:, 0:2 * QB], bank[:, 0:2 * QB], AF.Exp,
                        bias=negm[:, 0:1], scale=1.0 / 32.0),
                    waits=[(B.sMM, P.now(B.sMM)),
                           (B.sMM, av_tot.get((qb, kt - 2), 0))], inc=B.sA)
                for k2 in (kt - 1, kt):
                    eh = k2 % 2
                    P.op("tensor", lambda e, eh=eh, k2=k2: e.matmul(
                        ps.N[0:1, 0:QB], B.ones_bf[:, 0:1],
                        E_sb[:, eh * QB:(eh + 1) * QB],
                        start=(k2 == 0), stop=(k2 == NK - 1)),
                        waits=[(B.sA, exp_tot[(qb, kt)])], inc=B.sMM)
                    for j in range(8):
                        av_tot[(qb, kt)] = P.op("tensor", lambda e, j=j, k2=k2, eh=eh: e.matmul(
                            ps.C[j // 2][:, (j % 2) * QB:(j % 2 + 1) * QB],
                            V[:, k2 * D + j * 128:k2 * D + (j + 1) * 128],
                            E_sb[:, eh * QB:(eh + 1) * QB],
                            start=(k2 == 0 and j % 2 == 0), stop=(k2 == NK - 1)),
                            inc=B.sMM)
        mm_done = P.now(B.sMM)
        den_recip_v = P.op("vector", lambda e: e.reciprocal(recd[0:1, 0:QB], ps.N[0:1, 0:QB]),
                           waits=[(B.sMM, mm_done)], inc=B.sV)
        for i in range(2):
            mt = P.op("tensor", lambda e, i=i: e.transpose(
                ps.T[:, i * 128:(i + 1) * 128][:, 0:1], recd[0:1, i * 128:(i + 1) * 128],
                B.identity_f32[0:1, 0:1]),
                waits=[(B.sV, den_recip_v), (B.sV, outp_evac_a)], inc=B.sMM)
            P.op("vector", lambda e, i=i, mt=mt: e.tensor_copy(recdT[:, i:i + 1],
                                                               ps.T[:, i * 128:i * 128 + 1]),
                 waits=[(B.sMM, mt)], inc=B.sV)
        recdT_v = P.now(B.sV)
        for j in range(8):
            ctx_evac_v = P.op("vector", lambda e, j=j: e.tensor_copy(
                ctx_sb[:, j * QB:(j + 1) * QB], ps.C[j // 2][:, (j % 2) * QB:(j % 2 + 1) * QB]),
                waits=[(B.sMM, mm_done)], inc=B.sV)
        for qt in range(2):
            g_rt = qb * 2 + qt
            xld = P.op("sync", lambda e, g_rt=g_rt: e.dma_start(
                out=xrow[:, 0:D],
                in_=x_own[g_rt * 128:(g_rt + 1) * 128, :]),
                waits=[(B.sV, P.now(B.sV)), (B.sGD, P.now(B.sGD))], inc=B.sLD, inc_n=16)
            for ncol in range(2):
                mf = None
                for j in range(8):
                    w = ([(B.sV, ctx_evac_v), (B.sGD, wo_ld), (B.sV, recdT_v),
                          (B.sV, P.now(B.sV))] if j == 0 else [])
                    mf = P.op("tensor", lambda e, j=j, qt=qt, ncol=ncol: e.matmul(
                        ps.T[:, 0:512],
                        ctx_sb[:, j * QB + qt * 128:j * QB + (qt + 1) * 128],
                        Wb[:, j * D + ncol * 512:j * D + (ncol + 1) * 512],
                        start=(j == 0), stop=(j == 7)),
                        waits=w, inc=B.sMM)
                outp_evac_a = P.op("vector", lambda e, qt=qt, ncol=ncol, mf=mf: e.tensor_scalar(
                    attn_sb[:, ncol * 512:(ncol + 1) * 512], ps.T[:, 0:512],
                    recdT[:, qt:qt + 1], None, OP.mult),
                    waits=[(B.sMM, mf), (B.sV, recdT_v)], inc=B.sV)
            v1 = P.op("vector", lambda e: e.tensor_tensor(
                xrow[:, 0:D], xrow[:, 0:D], attn_sb[:, 0:D], OP.add),
                waits=[(B.sV, outp_evac_a), (B.sLD, xld)], inc=B.sV)
            v2 = P.op("vector", lambda e: e.tensor_tensor(
                xrow[:, 0:D], xrow[:, 0:D], brow_o[:, 0:D], OP.add),
                waits=[(B.sV, v1)], inc=B.sV)
            st_tot = P.op("gpsimd", lambda e, g_rt=g_rt: e.dma_start(
                out=x_out[g_rt * 128:(g_rt + 1) * 128, :], in_=xrow[:, 0:D]),
                waits=[(B.sV, v2)], inc=B.sGD, inc_n=16)

    P.op("gpsimd", lambda e: e.nop(), waits=[(B.sGD, st_tot)])




NH = 16
DH = 64
VSTRIDE = NH * (DH + 1)


def emit_rope(P, B, q3_f32, out_bf, ctab_t, stab_t, style, w_src):
    q4i = q3_f32.rearrange("p (h j two) -> p h j two", h=NH, two=2)
    s4i = B.scr_b.rearrange("p (h j two) -> p h j two", h=NH, two=2)
    q4h = q3_f32.rearrange("p (h half j) -> p h half j", h=NH, half=2)
    s4h = B.scr_b.rearrange("p (h half j) -> p h half j", h=NH, half=2)
    if style == "interleave":
        c1 = P.op("vector", lambda e: e.tensor_copy(s4i[:, :, :, 0:1], q4i[:, :, :, 1:2]),
                  waits=w_src, inc=B.sV)
        c2 = P.op("vector", lambda e: e.tensor_copy(s4i[:, :, :, 1:2], q4i[:, :, :, 0:1]),
                  waits=[(B.sV, c1)], inc=B.sV)
    else:
        c1 = P.op("vector", lambda e: e.tensor_copy(s4h[:, :, 0:1, :], q4h[:, :, 1:2, :]),
                  waits=w_src, inc=B.sV)
        c2 = P.op("vector", lambda e: e.tensor_copy(s4h[:, :, 1:2, :], q4h[:, :, 0:1, :]),
                  waits=[(B.sV, c1)], inc=B.sV)
    cb = ctab_t.rearrange("p (o j) -> p o j", o=1).broadcast_to([128, NH, DH])
    sb = stab_t.rearrange("p (o j) -> p o j", o=1).broadcast_to([128, NH, DH])
    q3 = q3_f32.rearrange("p (h j) -> p h j", h=NH)
    a3 = B.scr_a.rearrange("p (h j) -> p h j", h=NH)
    b3 = B.scr_b.rearrange("p (h j) -> p h j", h=NH)
    v1 = P.op("vector", lambda e: e.tensor_tensor(a3, q3, cb, OP.mult),
              waits=[(B.sV, c2)], inc=B.sV)
    v2 = P.op("vector", lambda e: e.tensor_tensor(b3, b3, sb, OP.mult),
              waits=[(B.sV, v1)], inc=B.sV)
    v3 = P.op("vector", lambda e: e.tensor_tensor(out_bf, B.scr_a[:, 0:D], B.scr_b[:, 0:D], OP.add),
              waits=[(B.sV, v2)], inc=B.sV)
    return v3


def emit_perhead_ln(P, B, q_f32, w_rep, b_rep, w_src, eps=LN_EPS):
    st = B.st16
    q3 = q_f32.rearrange("p (h j) -> p h j", h=NH)
    entryA = P.now(B.sA)
    a1 = P.op("scalar", lambda e: e.activation(B.scr_a[:, 0:D], q_f32, AF.Square),
              waits=list(w_src) + [(B.sV, P.now(B.sV))], inc=B.sA)
    v1 = P.op("vector", lambda e: e.reduce_sum(
        st[:, 0:NH], B.scr_a.rearrange("p (h j) -> p h j", h=NH), axis=AX.X),
        waits=[(B.sA, a1)], inc=B.sV)
    v2 = P.op("vector", lambda e: e.reduce_sum(st[:, NH:2 * NH], q3, axis=AX.X),
              waits=w_src, inc=B.sV)
    v3 = P.op("vector", lambda e: e.tensor_scalar(st[:, 2 * NH:3 * NH], st[:, NH:2 * NH],
                                                  1.0 / DH, None, OP.mult),
              waits=[(B.sV, v2)], inc=B.sV)
    v4 = P.op("vector", lambda e: e.tensor_tensor(st[:, 3 * NH:4 * NH], st[:, 2 * NH:3 * NH],
                                                  st[:, 2 * NH:3 * NH], OP.mult),
              waits=[(B.sV, v3)], inc=B.sV)
    v5 = P.op("vector", lambda e: e.scalar_tensor_tensor(st[:, 4 * NH:5 * NH], st[:, 0:NH],
                                                         1.0 / DH, st[:, 3 * NH:4 * NH],
                                                         OP.mult, OP.subtract),
              waits=[(B.sV, v4), (B.sV, v1)], inc=B.sV)
    v6 = P.op("vector", lambda e: e.tensor_scalar(st[:, 4 * NH:5 * NH], st[:, 4 * NH:5 * NH],
                                                  eps, None, OP.add),
              waits=[(B.sV, v5)], inc=B.sV)
    v7 = P.op("vector", lambda e: e.reciprocal(st[:, 5 * NH:6 * NH], st[:, 4 * NH:5 * NH]),
              waits=[(B.sV, v6)], inc=B.sV)
    a2 = P.op("scalar", lambda e: e.activation(st[:, 5 * NH:6 * NH], st[:, 5 * NH:6 * NH], AF.Sqrt),
              waits=[(B.sV, v7)], inc=B.sA)
    mb = st.rearrange("p (g h) -> p g h", g=6)
    meanb = mb[:, 2:3, :].rearrange("p o h -> p h o").broadcast_to([128, NH, DH])
    rstdb = mb[:, 5:6, :].rearrange("p o h -> p h o").broadcast_to([128, NH, DH])
    v8 = P.op("vector", lambda e: e.tensor_tensor(q3, q3, meanb, OP.subtract),
              waits=[(B.sV, v3), (B.sA, a2)], inc=B.sV)
    v9 = P.op("vector", lambda e: e.tensor_tensor(q3, q3, rstdb, OP.mult),
              waits=[(B.sV, v8)], inc=B.sV)
    wb = w_rep.rearrange("p (o j) -> p o j", o=1).broadcast_to([128, NH, DH])
    bb = b_rep.rearrange("p (o j) -> p o j", o=1).broadcast_to([128, NH, DH])
    v10 = P.op("vector", lambda e: e.tensor_tensor(q3, q3, wb, OP.mult),
               waits=[(B.sV, v9)], inc=B.sV)
    v11 = P.op("vector", lambda e: e.tensor_tensor(q3, q3, bb, OP.add),
               waits=[(B.sV, v10)], inc=B.sV)
    return v11


MHA_PARTS = {"av", "evac"}
MHA_CONST_BIAS = [False]
PROBE_NEGM = [False]
B3_DEBUG = [False]

def mha_attention(P, B, ps, KT, V_aug, qt_dram, QTe, QTo, E_sb, ctx64, ctx_pair, rep_sb,
                  recd64, negm, scale, negm_w, on_qb):
    zv = P.op("vector", lambda e: e.memset(QTe[64:128, :], 0.0),
              waits=[(B.sMM, P.now(B.sMM)), (B.sV, P.now(B.sV))], inc=B.sV)
    zv = P.op("vector", lambda e: e.memset(QTo[0:64, :], 0.0), inc=B.sV)
    lastexp = [0, 0]
    scores_last_qp = [0, 0]
    ctxmul_guard = zv
    repack_tot = 0
    outproj_done = 0
    for qb in range(NQ):
        qp = 0
        qld = P.op("sync", lambda e, qb=qb: e.dma_start(
            out=QTe[0:64, :].rearrange("p (o i r) -> p o i r", o=1, i=8),
            in_=qt_dram.rearrange("p (o i r) -> p o i r", o=1, i=8)[0:64, :, :, qb * QB:(qb + 1) * QB]),
            waits=[(B.sMM, scores_last_qp[0]), (B.sV, zv)], inc=B.sLD, inc_n=16)
        qld = P.op("sync", lambda e, qb=qb: e.dma_start(
            out=QTo[64:128, :].rearrange("p (o i r) -> p o i r", o=1, i=8),
            in_=qt_dram.rearrange("p (o i r) -> p o i r", o=1, i=8)[64:128, :, :, qb * QB:(qb + 1) * QB]),
            inc=B.sLD, inc_n=16)
        for hg in range(2):
            avdone = [0]
            def emit_av(h0, kt, hp, ex):
                for par in (range(2) if "av" in MHA_PARTS else []):
                    h = h0 + par
                    avdone[0] = P.op("tensor", lambda e, h=h, kt=kt, hp=hp, par=par: e.matmul(
                        ps.C[hp][0:65, par * QB:(par + 1) * QB],
                        V_aug[:, kt * VSTRIDE + h * (DH + 1):kt * VSTRIDE + h * (DH + 1) + 65],
                        E_sb[:, par * QB:(par + 1) * QB],
                        start=(kt == 0 and par == 0), stop=(kt == NK - 1)),
                        waits=[(B.sA, ex)], inc=B.sMM)
            pending = None
            for kt in range(NK):
                for hp in range(4):
                    h0 = hg * 8 + 2 * hp
                    j0 = h0 // 2
                    bank = [ps.A, ps.B][hp % 2]
                    w0 = [(B.sA, lastexp[hp % 2]), (B.sLD, qld), negm_w]
                    if kt == 0 and hp == 0:
                        w0.append((B.sV, ctxmul_guard))
                    m0 = P.op("tensor", lambda e, j0=j0, kt=kt, bank=bank: e.matmul(
                        bank[0:128, 0:QB],
                        KT[:, j0 * S + kt * 128:j0 * S + (kt + 1) * 128],
                        QTe[:, j0 * QB:(j0 + 1) * QB],
                        start=True, stop=True),
                        waits=w0, inc=B.sMM)
                    m1 = P.op("tensor", lambda e, j0=j0, kt=kt, bank=bank: e.matmul(
                        bank[0:128, QB:2 * QB],
                        KT[:, j0 * S + kt * 128:j0 * S + (kt + 1) * 128],
                        QTo[:, j0 * QB:(j0 + 1) * QB],
                        start=True, stop=True),
                        inc=B.sMM)
                    if pending is not None:
                        emit_av(*pending)
                    ex = P.op("scalar", lambda e, bank=bank: e.activation(
                        E_sb[:, 0:2 * QB], bank[:, 0:2 * QB], AF.Exp,
                        bias=negm[:, 0:1], scale=scale),
                        waits=[(B.sMM, P.now(B.sMM)), negm_w], inc=B.sA)
                    lastexp[hp % 2] = ex
                    pending = (h0, kt, hp, ex)
            if pending is not None:
                emit_av(*pending)
                pending = None
            avdone = avdone[0]
            scores_last_qp[0] = P.now(B.sMM)
            repguard = 0
            for hp in (range(4) if "evac" in MHA_PARTS else []):
                for par in range(2):
                    h8 = 2 * hp + par
                    h = hg * 8 + h8
                    vrec = P.op("vector", lambda e, hp=hp, par=par: e.reciprocal(
                        recd64[64:65, 0:QB], ps.C[hp][64:65, par * QB:(par + 1) * QB]),
                        waits=[(B.sMM, avdone), (B.sMM, repguard)], inc=B.sV)
                    mrep = P.op("tensor", lambda e: e.matmul(
                        ps.T[0:64, 0:QB], B.crow[64:65, 0:64], recd64[64:65, 0:QB],
                        start=True, stop=True),
                        waits=[(B.sV, vrec)], inc=B.sMM)
                    repguard = mrep
                    vrep = P.op("vector", lambda e: e.tensor_copy(rep_sb[0:64, 0:QB],
                                                                  ps.T[0:64, 0:QB]),
                                waits=[(B.sMM, mrep)], inc=B.sV)
                    ctxmul_guard = P.op("vector", lambda e, hp=hp, par=par, h=h: e.tensor_tensor(
                        ctx64[0:64, h * QB:(h + 1) * QB],
                        ps.C[hp][0:64, par * QB:(par + 1) * QB],
                        rep_sb[0:64, 0:QB], OP.mult),
                        waits=[(B.sV, vrep), (B.sGD, repack_tot)], inc=B.sV)
        c4 = ctx64.rearrange("p (j two q) -> p j two q", two=2, q=QB)
        cp4 = ctx_pair.rearrange("p (j q) -> p j q", q=QB)
        r1 = P.op("gpsimd", lambda e: e.dma_start(
            out=cp4[0:64, :, :], in_=c4[0:64, :, 0:1, :].rearrange("p j o q -> p j (o q)")),
            waits=[(B.sV, ctxmul_guard), (B.sMM, outproj_done)], inc=B.sGD, inc_n=16)
        repack_tot = P.op("gpsimd", lambda e: e.dma_start(
            out=cp4[64:128, :, :], in_=c4[0:64, :, 1:2, :].rearrange("p j o q -> p j (o q)")),
            inc=B.sGD, inc_n=16)
        outproj_done = on_qb(qb, repack_tot)
    return P.now(B.sMM)


def stage_rope_tables(nc, P, B, names):
    return {nm: nc.dram_tensor(nm, [rows, DH], F32, kind="ExternalInput").ap()
            for nm, rows in names}


def rope_tab_host(n, theta, style):
    pos = np.arange(n, dtype=np.float32)[:, None]
    inv = (1.0 / theta ** (np.arange(0, DH, 2, dtype=np.float64) / DH)).astype(np.float32)[None, :]
    ang = pos * inv
    c, s = np.cos(ang).astype(np.float32), np.sin(ang).astype(np.float32)
    C = np.zeros((n, DH), np.float32)
    Sg = np.zeros((n, DH), np.float32)
    if style == "interleave":
        C[:, 0::2] = c
        C[:, 1::2] = c
        Sg[:, 0::2] = -s
        Sg[:, 1::2] = s
    else:
        C[:, 0:32] = c
        C[:, 32:64] = c
        Sg[:, 0:32] = -s
        Sg[:, 32:64] = s
    return C, Sg


DEBUG_STAGE = 3

def emit_L1_attn_a(nc, P, B, ps, io):
    stage = 3
    pref = io["pref"]
    x_own = io["x_own"]
    x_ctx = io["x_ctx"]
    x_out = io["x_out"]
    qt_dram = io["qt_dram"]
    xpp_dram = io["xpp_dram"]
    wts = {n: nc.dram_tensor(pref + n, [D, D], F32, kind="ExternalInput").ap()
           for n in ("wq", "wk", "wv", "wo")}
    W1d = nc.dram_tensor(pref + "W1", [D, 4096], F32, kind="ExternalInput").ap()
    W2d = nc.dram_tensor(pref + "W2", [4096, D], F32, kind="ExternalInput").ap()
    bias = {n: nc.dram_tensor(pref + n, [1, D], F32, kind="ExternalInput").ap()
            for n in ("bq", "bk", "bv", "bo", "b2", "g1", "beta1", "g2", "beta2")}
    b1d = nc.dram_tensor(pref + "b1", [1, 4096], F32, kind="ExternalInput").ap()
    KT = B.KT
    V_aug = B.V_aug
    xg = B.xg
    Wb = B.Wb
    xn_bf = B.xn_bf
    qrot = B.xn_bf
    XTg = B.XTg
    brow = B.brow
    tab_c = B.tab_c
    tab_s = B.tab_s
    greps = B.greps
    nsc = B.nsc
    nsc_bf = B.nsc_bf
    negm = B.negm
    E_sb = B.E_sb
    sqn = B.E_sb[:, 0:512]
    ctx64 = B.ctx64
    ctx_pair = B.ctx_pair
    rep_sb = B.rep_sb
    recd64 = B.recd64
    attn_sb = B.attn_sb
    xrow = B.xrow
    b1col = B.b1col
    qts = B.ctx_pair[:, 0:8 * 128]
    qrow = B.attn_sb
    b2rep = B.scr_b
    P.op("vector", lambda e: e.memset(nsc[0:1, 0:4], 0.0),
         waits=[(B.sA, P.now(B.sA)), (B.sMM, P.now(B.sMM))], inc=B.sV)
    tabs = stage_rope_tables(nc, P, B, [(pref + "rc_ctx", S), (pref + "rs_ctx", S),
                                        (pref + "rc_own", R), (pref + "rs_own", R)])
    tabs = {k[len(pref):]: v for k, v in tabs.items()}

    def load_brow(name, waits=()):
        return P.op("gpsimd", lambda e: e.dma_start(out=brow[0:1, 0:D], in_=bias[name][:, :]),
                    waits=list(waits) + [(B.sMM, P.now(B.sMM))], inc=B.sGD, inc_n=16)
    for gi, n in enumerate(["g1", "beta1"]):
        gld = load_brow(n)
        for hhalf in range(2):
            emit_replicate_row(P, B, brow[0:1, hhalf * 512:(hhalf + 1) * 512],
                               greps[:, gi * D + hhalf * 512:gi * D + (hhalf + 1) * 512],
                               ps.N[:, 0:512], 512,
                               waits=[(B.sGD, gld), (B.sLD, P.now(B.sLD)),
                                      (B.sV, P.now(B.sV))])
    grep_v = P.now(B.sV)

    va4 = V_aug.rearrange("p (k h m) -> p k h m", k=NK, h=NH)
    ones_v = P.op("vector", lambda e: e.memset(va4[:, :, :, DH:DH + 1], 1.0), inc=B.sV)

    def row_pass(src_ap, n_rows, wname, bname, ctab, stab, dst):
        wld = P.op("gpsimd", lambda e: e.dma_start(
            out=Wb.rearrange("p (i n) -> p i n", i=8),
            in_=wts[wname].rearrange("(i p) n -> p i n", p=128)),
            waits=[(B.sMM, P.now(B.sMM))], inc=B.sGD, inc_n=16)
        bl = load_brow(bname)
        n_groups = n_rows // 512
        xg_guard = [[], []]
        qtdma_prev = [0]
        for g in range(n_groups):
            if dst in ("KT", "QT"):
                tld = P.op("sync", lambda e, g=g: e.dma_start(
                    out=tab_c.rearrange("p (t j) -> p t j", j=DH),
                    in_=ctab[g * 512:(g + 1) * 512, :].rearrange("(t p) j -> p t j", p=128)),
                    waits=[(B.sV, P.now(B.sV))], inc=B.sLD, inc_n=16)
                tld = P.op("sync", lambda e, g=g: e.dma_start(
                    out=tab_s.rearrange("p (t j) -> p t j", j=DH),
                    in_=stab[g * 512:(g + 1) * 512, :].rearrange("(t p) j -> p t j", p=128)),
                    inc=B.sLD, inc_n=16)
            else:
                tld = 0
            xt_v = 0
            for t in range(4):
                sl = 0
                gt = g * 4 + t
                ld = P.op("sync", lambda e, gt=gt: e.dma_start(
                    out=xg[:, 0:D],
                    in_=src_ap[gt * 128:(gt + 1) * 128, :]),
                    waits=xg_guard[sl], inc=B.sLD, inc_n=16)
                aw, vL = emit_ln(P, B, xg[:, 0:D], B.scr_ln[:, 0:D],
                                 [(B.sLD, ld), (B.sV, grep_v)])
                aw2, vL2 = emit_ln(P, B, B.scr_ln[:, 0:D], xn_bf[:, 0:D],
                                   [aw], gain_rep=greps[:, 0:D], bias_rep=greps[:, D:2 * D])
                for i in range(8):
                    xt_v = emit_transpose(P, B, xn_bf[:, i * 128:(i + 1) * 128],
                                          XTg[:, i * 512 + t * 128:i * 512 + (t + 1) * 128],
                                          tr_slot(ps, t * 8 + i),
                                          waits=[(B.sV, vL2)])
                xg_guard[sl] = [(B.sV, xt_v), (B.sA, P.now(B.sA))]
            for rt in range(4):
                g_rt = g * 4 + rt
                bank = [ps.A, ps.B][rt % 2]
                for ncol in range(2):
                    mf = None
                    for i in range(8):
                        w = ([(B.sV, xt_v), (B.sGD, wld), (B.sV, P.now(B.sV))]
                             if i == 0 else [])
                        mf = P.op("tensor", lambda e, i=i, rt=rt, ncol=ncol, bank=bank: e.matmul(
                            bank[:, 0:512],
                            XTg[:, i * 512 + rt * 128:i * 512 + (rt + 1) * 128],
                            Wb[:, i * D + ncol * 512:i * D + (ncol + 1) * 512],
                            start=(i == 0), stop=False),
                            waits=w, inc=B.sMM)
                    mf = P.op("tensor", lambda e, ncol=ncol, bank=bank: e.matmul(
                        bank[:, 0:512], B.ones_bf[0:1, :],
                        brow[0:1, ncol * 512:(ncol + 1) * 512],
                        start=False, stop=True),
                        waits=[(B.sGD, bl)], inc=B.sMM)
                    if dst == "V":
                        kt_i = g_rt
                        dst_ap = V_aug.rearrange("p (k h m) -> p k h m", k=NK, h=NH)[
                            :, kt_i:kt_i + 1, ncol * 8:(ncol + 1) * 8, 0:DH]
                        P.op("vector", lambda e, dst_ap=dst_ap, bank=bank: e.tensor_copy(
                            dst_ap, bank[:, 0:512].rearrange("p (o h m) -> p o h m", o=1, h=8)),
                            waits=[(B.sMM, mf), (B.sV, ones_v)], inc=B.sV)
                    else:
                        P.op("vector", lambda e, ncol=ncol, bank=bank: e.tensor_copy(
                            qrow[:, ncol * 512:(ncol + 1) * 512], bank[:, 0:512]),
                            waits=[(B.sMM, mf), (B.sV, P.now(B.sV)),
                                   (B.sA, P.now(B.sA))], inc=B.sV)
                if dst in ("KT", "QT"):
                    rv = emit_rope(P, B, qrow[:, 0:D], qrot[:, 0:D],
                                   tab_c[:, rt * DH:(rt + 1) * DH],
                                   tab_s[:, rt * DH:(rt + 1) * DH],
                                   rope_style, [(B.sV, P.now(B.sV)), (B.sLD, tld)])
                    for i in range(8):
                        if dst == "KT":
                            dsl = KT[:, i * S + g_rt * 128:i * S + (g_rt + 1) * 128]
                            ew = ()
                        else:
                            dsl = qts[:, i * 128:(i + 1) * 128]
                            ew = [(B.sLD, qtdma_prev[0])]
                        emit_transpose(P, B, qrot[:, i * 128:(i + 1) * 128], dsl,
                                       tr_slot(ps, i), waits=[(B.sV, rv)], evac_waits=ew)
                    if dst == "QT":
                        nmm = 0
                        for i in range(8):
                            sq = P.op("vector", lambda e, i=i: e.tensor_tensor(
                                sqn[:, 0:128], qts[:, i * 128:(i + 1) * 128],
                                qts[:, i * 128:(i + 1) * 128], OP.mult),
                                waits=[(B.sV, P.now(B.sV)), (B.sMM, nmm)], inc=B.sV)
                            nmm = P.op("tensor", lambda e, i=i: e.matmul(
                                ps.N[0:1, 0:128], B.ones_bf[:, 0:1], sqn[:, 0:128],
                                start=True, stop=True),
                                waits=[(B.sV, sq)], inc=B.sMM)
                            rv_ = P.op("vector", lambda e: e.reduce_max(nsc[0:1, 2:3],
                                                                  ps.N[0:1, 0:128], axis=AX.X),
                                 waits=[(B.sMM, nmm)], inc=B.sV)
                            P.op("vector", lambda e: e.tensor_tensor(
                                nsc[0:1, 0:1], nsc[0:1, 0:1], nsc[0:1, 2:3], OP.max),
                                waits=[(B.sV, rv_)], inc=B.sV)
                            nmm = P.now(B.sMM)
                        qtdma_prev[0] = P.op("sync", lambda e, g_rt=g_rt: e.dma_start(
                            out=qt_dram.rearrange("p (i r) -> p i r", i=8)[:, :, g_rt * 128:(g_rt + 1) * 128],
                            in_=qts.rearrange("p (i r) -> p i r", i=8)),
                            waits=[(B.sV, P.now(B.sV))], inc=B.sLD, inc_n=16)
            if dst == "KT":
                for j in range(8):
                    tgt = KT[:, j * S + g * 512:j * S + (g + 1) * 512]
                    sq = P.op("vector", lambda e, tgt=tgt: e.tensor_tensor(
                        sqn[:, 0:512], tgt, tgt, OP.mult),
                        waits=[(B.sV, P.now(B.sV)), (B.sMM, P.now(B.sMM))], inc=B.sV)
                    nmm = P.op("tensor", lambda e, j=j: e.matmul(
                        ps.N[0:1, 0:512], B.ones_bf[:, 0:1], sqn[:, 0:512],
                        start=True, stop=True),
                        waits=[(B.sV, sq)], inc=B.sMM)
                    rv_ = P.op("vector", lambda e: e.reduce_max(nsc[0:1, 2:3], ps.N[0:1, 0:512], axis=AX.X),
                         waits=[(B.sMM, nmm)], inc=B.sV)
                    P.op("vector", lambda e: e.tensor_tensor(
                        nsc[0:1, 1:2], nsc[0:1, 1:2], nsc[0:1, 2:3], OP.max),
                        waits=[(B.sV, rv_)], inc=B.sV)

    rope_style = "interleave"
    row_pass(x_ctx, S, "wk", "bk", tabs["rc_ctx"], tabs["rs_ctx"], "KT")
    row_pass(x_ctx, S, "wv", "bv", None, None, "V")
    row_pass(x_own, R, "wq", "bq", tabs["rc_own"], tabs["rs_own"], "QT")

    vm = P.op("vector", lambda e: e.tensor_tensor(nsc[0:1, 2:3], nsc[0:1, 0:1],
                                                  nsc[0:1, 1:2], OP.mult),
              waits=[(B.sV, P.now(B.sV))], inc=B.sV)
    am = P.op("scalar", lambda e: e.activation(nsc[0:1, 3:4], nsc[0:1, 2:3], AF.Sqrt),
              waits=[(B.sV, vm)], inc=B.sA)
    vneg = P.op("vector", lambda e: e.tensor_scalar(nsc_bf[0:1, 0:1], nsc[0:1, 3:4],
                                                    -1.0 / 8.0, None, OP.mult),
                waits=[(B.sA, am)], inc=B.sV)
    mrep = P.op("tensor", lambda e: e.matmul(ps.N[:, 0:1], B.ones_bf[0:1, :],
                                             nsc_bf[0:1, 0:1], start=True, stop=True),
                waits=[(B.sV, vneg)], inc=B.sMM)
    vnegm = P.op("vector", lambda e: e.tensor_copy(negm[:, 0:1], ps.N[:, 0:1]),
                 waits=[(B.sMM, mrep)], inc=B.sV)

    bo_l = load_brow("bo")

    wo_ld = P.op("gpsimd", lambda e: e.dma_start(
        out=Wb.rearrange("p (i n) -> p i n", i=8),
        in_=wts["wo"].rearrange("(i p) n -> p i n", p=128)),
        waits=[(B.sMM, P.now(B.sMM))], inc=B.sGD, inc_n=16)

    xt2_marks = []

    def on_qb(qb, repack_gd):
        last_mm = 0
        for qt in range(2):
            g_rt = qb * 2 + qt
            xld = P.op("sync", lambda e, g_rt=g_rt: e.dma_start(
                out=xrow[:, 0:D], in_=x_own[g_rt * 128:(g_rt + 1) * 128, :]),
                waits=[(B.sV, P.now(B.sV)), (B.sGD, P.now(B.sGD))], inc=B.sLD, inc_n=16)
            for ncol in range(2):
                mf = None
                for j in range(8):
                    w = ([(B.sGD, repack_gd), (B.sGD, wo_ld), (B.sV, P.now(B.sV)),
                          (B.sA, P.now(B.sA))] if j == 0 else [])
                    mf = P.op("tensor", lambda e, j=j, qt=qt, ncol=ncol: e.matmul(
                        ps.T[:, 0:512],
                        ctx_pair[:, j * QB + qt * 128:j * QB + (qt + 1) * 128],
                        Wb[:, j * D + ncol * 512:j * D + (ncol + 1) * 512],
                        start=(j == 0), stop=False),
                        waits=w, inc=B.sMM)
                mf = P.op("tensor", lambda e, ncol=ncol: e.matmul(
                    ps.T[:, 0:512], B.ones_bf[0:1, :],
                    brow[0:1, ncol * 512:(ncol + 1) * 512],
                    start=False, stop=True),
                    waits=[(B.sGD, bo_l)], inc=B.sMM)
                last_mm = mf
                P.op("vector", lambda e, ncol=ncol, mf=mf: e.tensor_copy(
                    attn_sb[:, ncol * 512:(ncol + 1) * 512], ps.T[:, 0:512]),
                    waits=[(B.sMM, mf), (B.sGD, P.now(B.sGD))], inc=B.sV)
            aw, vL = emit_ln(P, B, xrow[:, 0:D], B.scr_ln[:, 0:D],
                             [(B.sLD, xld), (B.sV, P.now(B.sV))])
            v1 = P.op("vector", lambda e: e.tensor_tensor(
                attn_sb[:, 0:D], attn_sb[:, 0:D], B.scr_ln[:, 0:D], OP.add),
                waits=[aw, (B.sV, vL)], inc=B.sV)
            v2 = P.op("vector", lambda e: e.tensor_tensor(
                xrow[:, 0:D], xrow[:, 0:D], attn_sb[:, 0:D], OP.add),
                waits=[(B.sV, v1)], inc=B.sV)
            stq = P.op("gpsimd", lambda e, g_rt=g_rt: e.dma_start(
                out=x_out[g_rt * 128:(g_rt + 1) * 128, :], in_=xrow[:, 0:D]),
                waits=[(B.sV, v2)], inc=B.sGD, inc_n=16)
            stq2 = P.op("gpsimd", lambda e, g_rt=g_rt: e.dma_start(
                out=xpp_dram[g_rt * 128:(g_rt + 1) * 128, :], in_=attn_sb[:, 0:D]),
                waits=[(B.sV, v2)], inc=B.sGD, inc_n=16)
            xt2_marks.append(stq2)
        return last_mm

    if stage >= 2:
        def on_qb_stub(qb, repack_gd):
            last = 0
            for qt in range(2):
                g_rt = qb * 2 + qt
                xld = P.op("sync", lambda e, g_rt=g_rt: e.dma_start(
                    out=xrow[:, 0:D], in_=x_own[g_rt * 128:(g_rt + 1) * 128, :]),
                    waits=[(B.sGD, P.now(B.sGD)), (B.sV, P.now(B.sV))], inc=B.sLD, inc_n=16)
                v2 = P.op("vector", lambda e: e.tensor_copy(attn_sb[:, 0:D], xrow[:, 0:D]),
                          waits=[(B.sLD, xld), (B.sV, P.now(B.sV))], inc=B.sV)
                P.op("gpsimd", lambda e, g_rt=g_rt: e.dma_start(
                    out=x_out[g_rt * 128:(g_rt + 1) * 128, :], in_=xrow[:, 0:D]),
                    waits=[(B.sV, v2)], inc=B.sGD, inc_n=16)
                P.op("gpsimd", lambda e, g_rt=g_rt: e.dma_start(
                    out=xpp_dram[g_rt * 128:(g_rt + 1) * 128, :], in_=attn_sb[:, 0:D]),
                    inc=B.sGD, inc_n=16)
            return 0
        QTe = XTg[:, 0:8 * QB]
        QTo = XTg[:, 8 * QB:16 * QB]
        mha_attention(P, B, ps, KT, V_aug, qt_dram, QTe, QTo, E_sb, ctx64, ctx_pair, rep_sb,
                      recd64, negm, 1.0 / 8.0, (B.sV, vnegm),
                      on_qb if stage != 21 else on_qb_stub)
    else:
        for g_rt in range(R // 128):
            xld = P.op("sync", lambda e, g_rt=g_rt: e.dma_start(
                out=xrow[:, 0:D], in_=x_own[g_rt * 128:(g_rt + 1) * 128, :]),
                waits=[(B.sGD, P.now(B.sGD))], inc=B.sLD, inc_n=16)
            v2 = P.op("vector", lambda e: e.tensor_copy(attn_sb[:, 0:D], xrow[:, 0:D]),
                      waits=[(B.sLD, xld), (B.sV, P.now(B.sV)), (B.sA, P.now(B.sA)),
                             (B.sMM, P.now(B.sMM))], inc=B.sV)
            P.op("gpsimd", lambda e, g_rt=g_rt: e.dma_start(
                out=x_out[g_rt * 128:(g_rt + 1) * 128, :], in_=xrow[:, 0:D]),
                waits=[(B.sV, v2)], inc=B.sGD, inc_n=16)
            P.op("gpsimd", lambda e, g_rt=g_rt: e.dma_start(
                out=xpp_dram[g_rt * 128:(g_rt + 1) * 128, :], in_=attn_sb[:, 0:D]),
                inc=B.sGD, inc_n=16)

    W1s = KT[:, 0:8 * 2048]
    W2s = [KT[:, 8 * 2048 + dh * 16 * 512: 8 * 2048 + (dh + 1) * 16 * 512]
           for dh in range(2)]
    h_bf = V_aug[:, 0:16 * 512]
    for gi, n in enumerate(["g2", "beta2"]):
        gl2 = load_brow(n, waits=[(B.sMM, P.now(B.sMM)), (B.sV, P.now(B.sV))])
        for hhalf in range(2):
            emit_replicate_row(P, B, brow[0:1, hhalf * 512:(hhalf + 1) * 512],
                               greps[:, gi * D + hhalf * 512:gi * D + (hhalf + 1) * 512],
                               ps.N[:, 0:512], 512,
                               waits=[(B.sGD, gl2), (B.sV, P.now(B.sV))])
    bl2 = load_brow("b2", waits=[(B.sV, P.now(B.sV))])
    for hhalf in range(2):
        emit_replicate_row(P, B, brow[0:1, hhalf * 512:(hhalf + 1) * 512],
                           b2rep[:, hhalf * 512:(hhalf + 1) * 512], ps.N[:, 0:512], 512,
                           waits=[(B.sGD, bl2), (B.sV, P.now(B.sV))])
    st_tot = 0
    for fs in (range(2) if stage >= 3 else []):
        w1ld = P.op("gpsimd", lambda e, fs=fs: e.dma_start(
            out=W1s.rearrange("p (i f) -> p i f", i=8),
            in_=W1d[:, fs * 2048:(fs + 1) * 2048].rearrange("(i p) f -> p i f", p=128)),
            waits=[(B.sMM, P.now(B.sMM))], inc=B.sGD, inc_n=16)
        w2ld = 0
        for dh in range(2):
            w2ld = P.op("gpsimd", lambda e, fs=fs, dh=dh: e.dma_start(
                out=W2s[dh].rearrange("p (i f) -> p i f", i=16),
                in_=W2d[fs * 2048:(fs + 1) * 2048, dh * 512:(dh + 1) * 512].rearrange(
                    "(i p) f -> p i f", p=128)),
                inc=B.sGD, inc_n=16)
        bcld = P.op("gpsimd", lambda e, fs=fs: e.dma_start(
            out=b1col[:, 0:16].rearrange("p (o t) -> p o t", o=1),
            in_=b1d[0:1, fs * 2048:(fs + 1) * 2048].rearrange("o (t p) -> p o t", p=128),
            allow_slow_non_contiguous=True),
            waits=[(B.sV, P.now(B.sV))], inc=B.sGD, inc_n=16)
        for rc in range(4):
            xt_v = 0
            xgg = [[(B.sV, P.now(B.sV)), (B.sA, P.now(B.sA)),
                    (B.sGD, P.now(B.sGD)), (B.sMM, P.now(B.sMM))]] * 2
            for t in range(4):
                sl = 0
                gt = rc * 4 + t
                ldx = P.op("sync", lambda e, gt=gt: e.dma_start(
                    out=xg[:, 0:D],
                    in_=xpp_dram[gt * 128:(gt + 1) * 128, :]),
                    waits=xgg[sl], inc=B.sLD, inc_n=16)
                aw2, vL2 = emit_ln(P, B, xg[:, 0:D], xn_bf[:, 0:D],
                                   [(B.sLD, ldx)], gain_rep=greps[:, 0:D],
                                   bias_rep=greps[:, D:2 * D])
                for i in range(8):
                    xt_v = emit_transpose(P, B, xn_bf[:, i * 128:(i + 1) * 128],
                                          XTg[:, i * 512 + t * 128:i * 512 + (t + 1) * 128],
                                          tr_slot(ps, t * 8 + i), waits=[(B.sV, vL2)])
                xgg[sl] = [(B.sV, xt_v), (B.sA, P.now(B.sA))]
            for fm in range(16):
                bank = [ps.A, ps.B][fm % 2]
                mf = None
                for i in range(8):
                    w = ([(B.sGD, w1ld), (B.sGD, bcld), (B.sV, xt_v),
                          (B.sV, P.now(B.sV))] if i == 0 else [])
                    mf = P.op("tensor", lambda e, i=i, fm=fm, bank=bank: e.matmul(
                        bank[:, 0:512],
                        W1s[:, i * 2048 + fm * 128:i * 2048 + (fm + 1) * 128],
                        XTg[:, i * 512:(i + 1) * 512],
                        start=(i == 0), stop=(i == 7)),
                        waits=w, inc=B.sMM)
                P.op("vector", lambda e, fm=fm, bank=bank: e.tensor_scalar(
                    h_bf[:, fm * 512:(fm + 1) * 512], bank[:, 0:512],
                    b1col[:, fm:fm + 1], 0.0, OP.add, OP.max),
                    waits=[(B.sMM, mf)], inc=B.sV)
            h_v = P.now(B.sV)
            for dh in range(2):
                for rt in range(4):
                    g_rt = rc * 4 + rt
                    mf = None
                    for i in range(16):
                        w = ([(B.sV, h_v), (B.sGD, w2ld), (B.sV, P.now(B.sV))]
                             if i == 0 else [])
                        mf = P.op("tensor", lambda e, i=i, rt=rt, dh=dh: e.matmul(
                            ps.C[rt][:, 0:512],
                            h_bf[:, i * 512 + rt * 128:i * 512 + (rt + 1) * 128],
                            W2s[dh][:, i * 512:(i + 1) * 512],
                            start=(i == 0), stop=(i == 15)),
                            waits=w, inc=B.sMM)
                    xld = P.op("sync", lambda e, g_rt=g_rt: e.dma_start(
                        out=xrow[:, 0:D], in_=x_out[g_rt * 128:(g_rt + 1) * 128, :]),
                        waits=[(B.sGD, P.now(B.sGD)), (B.sV, P.now(B.sV))],
                        inc=B.sLD, inc_n=16)
                    if fs == 1:
                        v1 = P.op("vector", lambda e, rt=rt, dh=dh: e.tensor_tensor(
                            attn_sb[:, 0:512], ps.C[rt][:, 0:512],
                            b2rep[:, dh * 512:(dh + 1) * 512], OP.add),
                            waits=[(B.sMM, mf), (B.sLD, xld)], inc=B.sV)
                    else:
                        v1 = P.op("vector", lambda e, rt=rt: e.tensor_copy(
                            attn_sb[:, 0:512], ps.C[rt][:, 0:512]),
                            waits=[(B.sMM, mf), (B.sLD, xld)], inc=B.sV)
                    v2 = P.op("vector", lambda e, dh=dh: e.tensor_tensor(
                        xrow[:, dh * 512:(dh + 1) * 512], xrow[:, dh * 512:(dh + 1) * 512],
                        attn_sb[:, 0:512], OP.add),
                        waits=[(B.sV, v1)], inc=B.sV)
                    st_tot = P.op("gpsimd", lambda e, g_rt=g_rt: e.dma_start(
                        out=x_out[g_rt * 128:(g_rt + 1) * 128, :], in_=xrow[:, 0:D]),
                        waits=[(B.sV, v2)], inc=B.sGD, inc_n=16)
    P.op("gpsimd", lambda e: e.nop(), waits=[(B.sGD, P.now(B.sGD))])





def emit_L0_attn_b(nc, P, B, ps, io):
    pref = io["pref"]
    x_own = io["x_own"]
    x_ctx = io["x_ctx"]
    x_out = io["x_out"]
    qt_dram = io["qt_dram"]
    wts = {n: nc.dram_tensor(pref + n, [D, D], F32, kind="ExternalInput").ap()
           for n in ("wqT", "wkT", "wvT", "woT")}
    bias = {n: nc.dram_tensor(pref + n, [1, D if n == "bo" else DH], F32, kind="ExternalInput").ap()
            for n in ("bo", "qn_w", "qn_b", "kn_w", "kn_b")}
    KT = B.KT
    V_aug = B.V_aug
    xg = B.xg
    Wb = B.Wb
    xn_bf = B.xn_bf
    qrot = B.xn_bf
    XTg = B.XTg
    brow = B.brow
    tab_c = B.tab_c
    tab_s = B.tab_s
    hreps = B.hreps
    borep = B.borep
    nsc = B.nsc
    nsc_bf = B.nsc_bf
    negm = B.negm
    E_sb = B.E_sb
    sqn = B.E_sb[:, 0:512]
    ctx64 = B.ctx64
    ctx_pair = B.ctx_pair
    rep_sb = B.rep_sb
    recd64 = B.recd64
    attn_sb = B.attn_sb
    xrow = B.xrow
    qts = B.ctx_pair[:, 0:8 * 128]
    qrow = B.attn_sb
    P.op("vector", lambda e: e.memset(nsc[0:1, 0:4], 0.0),
         waits=[(B.sA, P.now(B.sA)), (B.sMM, P.now(B.sMM))], inc=B.sV)
    tabs = stage_rope_tables(nc, P, B, [(pref + "rc_ctx", S), (pref + "rs_ctx", S),
                                        (pref + "rc_own", R), (pref + "rs_own", R)])
    tabs = {k[len(pref):]: v for k, v in tabs.items()}

    def load_brow(name, waits=()):
        return P.op("gpsimd", lambda e: e.dma_start(out=brow[0:1, 0:bias[name].shape[1]],
                                                    in_=bias[name][:, :]),
                    waits=list(waits) + [(B.sMM, P.now(B.sMM))], inc=B.sGD, inc_n=16)

    for gi, n in enumerate(["qn_w", "qn_b", "kn_w", "kn_b"]):
        gld = load_brow(n)
        emit_replicate_row(P, B, brow[0:1, 0:DH], hreps[:, gi * DH:(gi + 1) * DH],
                           ps.N[:, 0:DH], DH,
                           waits=[(B.sGD, gld), (B.sLD, P.now(B.sLD)), (B.sV, P.now(B.sV))])
    grep_v = P.now(B.sV)

    va4 = V_aug.rearrange("p (k h m) -> p k h m", k=NK, h=NH)
    ones_v = P.op("vector", lambda e: e.memset(va4[:, :, :, DH:DH + 1], 1.0), inc=B.sV)

    def row_pass(src_ap, n_rows, wname, ctab, stab, dst, hrep_off):
        wld = P.op("gpsimd", lambda e: e.dma_start(
            out=Wb.rearrange("p (i n) -> p i n", i=8),
            in_=wts[wname].rearrange("(i p) n -> p i n", p=128)),
            waits=[(B.sMM, P.now(B.sMM))], inc=B.sGD, inc_n=16)
        n_groups = n_rows // 512
        xg_guard = [[]]
        qtdma_prev = [0]
        for g in range(n_groups):
            if dst in ("KT", "QT"):
                tld = P.op("sync", lambda e, g=g: e.dma_start(
                    out=tab_c.rearrange("p (t j) -> p t j", j=DH),
                    in_=ctab[g * 512:(g + 1) * 512, :].rearrange("(t p) j -> p t j", p=128)),
                    waits=[(B.sV, P.now(B.sV))], inc=B.sLD, inc_n=16)
                tld = P.op("sync", lambda e, g=g: e.dma_start(
                    out=tab_s.rearrange("p (t j) -> p t j", j=DH),
                    in_=stab[g * 512:(g + 1) * 512, :].rearrange("(t p) j -> p t j", p=128)),
                    inc=B.sLD, inc_n=16)
            else:
                tld = 0
            xt_v = 0
            for t in range(4):
                gt = g * 4 + t
                ld = P.op("sync", lambda e, gt=gt: e.dma_start(
                    out=xg[:, 0:D], in_=src_ap[gt * 128:(gt + 1) * 128, :]),
                    waits=xg_guard[0], inc=B.sLD, inc_n=16)
                aw, vL = emit_ln(P, B, xg[:, 0:D], xn_bf[:, 0:D], [(B.sLD, ld)])
                for i in range(8):
                    xt_v = emit_transpose(P, B, xn_bf[:, i * 128:(i + 1) * 128],
                                          XTg[:, i * 512 + t * 128:i * 512 + (t + 1) * 128],
                                          tr_slot(ps, t * 8 + i), waits=[aw])
                xg_guard[0] = [(B.sV, xt_v), (B.sA, P.now(B.sA))]
            for rt in range(4):
                g_rt = g * 4 + rt
                bank = [ps.A, ps.B][rt % 2]
                for ncol in range(2):
                    mf = None
                    for i in range(8):
                        w = ([(B.sV, xt_v), (B.sGD, wld), (B.sV, P.now(B.sV))]
                             if i == 0 else [])
                        mf = P.op("tensor", lambda e, i=i, rt=rt, ncol=ncol, bank=bank: e.matmul(
                            bank[:, 0:512],
                            XTg[:, i * 512 + rt * 128:i * 512 + (rt + 1) * 128],
                            Wb[:, i * D + ncol * 512:i * D + (ncol + 1) * 512],
                            start=(i == 0), stop=(i == 7)),
                            waits=w, inc=B.sMM)
                    if dst == "V":
                        kt_i = g_rt
                        dst_ap = V_aug.rearrange("p (k h m) -> p k h m", k=NK, h=NH)[
                            :, kt_i:kt_i + 1, ncol * 8:(ncol + 1) * 8, 0:DH]
                        P.op("vector", lambda e, dst_ap=dst_ap, bank=bank: e.tensor_copy(
                            dst_ap, bank[:, 0:512].rearrange("p (o h m) -> p o h m", o=1, h=8)),
                            waits=[(B.sMM, mf), (B.sV, ones_v)], inc=B.sV)
                    else:
                        P.op("vector", lambda e, ncol=ncol, bank=bank: e.tensor_copy(
                            qrow[:, ncol * 512:(ncol + 1) * 512], bank[:, 0:512]),
                            waits=[(B.sMM, mf), (B.sV, P.now(B.sV)),
                                   (B.sA, P.now(B.sA)), (B.sGD, P.now(B.sGD))], inc=B.sV)
                if dst in ("KT", "QT"):
                    phv = emit_perhead_ln(P, B, qrow[:, 0:D],
                                          hreps[:, hrep_off * DH:(hrep_off + 1) * DH],
                                          hreps[:, (hrep_off + 1) * DH:(hrep_off + 2) * DH],
                                          [(B.sV, P.now(B.sV)), (B.sV, grep_v)])
                    rv = emit_rope(P, B, qrow[:, 0:D], qrot[:, 0:D],
                                   tab_c[:, rt * DH:(rt + 1) * DH],
                                   tab_s[:, rt * DH:(rt + 1) * DH],
                                   "half", [(B.sV, phv), (B.sLD, tld)])
                    for i in range(8):
                        if dst == "KT":
                            dsl = KT[:, i * S + g_rt * 128:i * S + (g_rt + 1) * 128]
                            ew = ()
                        else:
                            dsl = qts[:, i * 128:(i + 1) * 128]
                            ew = [(B.sLD, qtdma_prev[0])]
                        emit_transpose(P, B, qrot[:, i * 128:(i + 1) * 128], dsl,
                                       tr_slot(ps, i), waits=[(B.sV, rv)], evac_waits=ew)
                    if dst == "QT":
                        nmm = 0
                        for i in range(8):
                            sq = P.op("vector", lambda e, i=i: e.tensor_tensor(
                                sqn[:, 0:128], qts[:, i * 128:(i + 1) * 128],
                                qts[:, i * 128:(i + 1) * 128], OP.mult),
                                waits=[(B.sV, P.now(B.sV)), (B.sMM, nmm)], inc=B.sV)
                            nmm = P.op("tensor", lambda e, i=i: e.matmul(
                                ps.N[0:1, 0:128], B.ones_bf[:, 0:1], sqn[:, 0:128],
                                start=True, stop=True),
                                waits=[(B.sV, sq)], inc=B.sMM)
                            rv_ = P.op("vector", lambda e: e.reduce_max(nsc[0:1, 2:3],
                                                                  ps.N[0:1, 0:128], axis=AX.X),
                                 waits=[(B.sMM, nmm)], inc=B.sV)
                            P.op("vector", lambda e: e.tensor_tensor(
                                nsc[0:1, 0:1], nsc[0:1, 0:1], nsc[0:1, 2:3], OP.max),
                                waits=[(B.sV, rv_)], inc=B.sV)
                            nmm = P.now(B.sMM)
                        qtdma_prev[0] = P.op("sync", lambda e, g_rt=g_rt: e.dma_start(
                            out=qt_dram.rearrange("p (i r) -> p i r", i=8)[:, :, g_rt * 128:(g_rt + 1) * 128],
                            in_=qts.rearrange("p (i r) -> p i r", i=8)),
                            waits=[(B.sV, P.now(B.sV))], inc=B.sLD, inc_n=16)
            if dst == "KT":
                for j in range(8):
                    tgt = KT[:, j * S + g * 512:j * S + (g + 1) * 512]
                    sq = P.op("vector", lambda e, tgt=tgt: e.tensor_tensor(
                        sqn[:, 0:512], tgt, tgt, OP.mult),
                        waits=[(B.sV, P.now(B.sV)), (B.sMM, P.now(B.sMM))], inc=B.sV)
                    nmm = P.op("tensor", lambda e, j=j: e.matmul(
                        ps.N[0:1, 0:512], B.ones_bf[:, 0:1], sqn[:, 0:512],
                        start=True, stop=True),
                        waits=[(B.sV, sq)], inc=B.sMM)
                    rv_ = P.op("vector", lambda e: e.reduce_max(nsc[0:1, 2:3], ps.N[0:1, 0:512], axis=AX.X),
                         waits=[(B.sMM, nmm)], inc=B.sV)
                    if B3_DEBUG[0]:
                        P.op("vector", lambda e, g=g, j=j: e.tensor_copy(
                            B.scr_c[0:1, g * 8 + j:g * 8 + j + 1], nsc[0:1, 2:3]), inc=B.sV)
                        if g == 0 and j == 0:
                            dsq = nc.dram_tensor("dbg_sqn00", [128, 512], BF16,
                                                 kind="ExternalOutput").ap()
                            gd = P.op("gpsimd", lambda e, dsq=dsq: e.dma_start(
                                out=dsq[:, :], in_=sqn[:, 0:512]),
                                waits=[(B.sMM, nmm)], inc=B.sGD, inc_n=16)
                            dkt0 = nc.dram_tensor("dbg_kt00", [128, 512], BF16,
                                                  kind="ExternalOutput").ap()
                            gd = P.op("gpsimd", lambda e, dkt0=dkt0, tgt=tgt: e.dma_start(
                                out=dkt0[:, :], in_=tgt),
                                waits=[(B.sMM, nmm)], inc=B.sGD, inc_n=16)
                            P.op("vector", lambda e: e.tensor_copy(
                                B.greps[0:1, 0:512], ps.N[0:1, 0:512]),
                                waits=[(B.sGD, gd)], inc=B.sV)
                    P.op("vector", lambda e: e.tensor_tensor(
                        nsc[0:1, 1:2], nsc[0:1, 1:2], nsc[0:1, 2:3], OP.max),
                        waits=[(B.sV, rv_)], inc=B.sV)

    row_pass(x_ctx, S, "wkT", tabs["rc_ctx"], tabs["rs_ctx"], "KT", 2)
    row_pass(x_ctx, S, "wvT", None, None, "V", 0)
    row_pass(x_own, R, "wqT", tabs["rc_own"], tabs["rs_own"], "QT", 0)

    vm = P.op("vector", lambda e: e.tensor_tensor(nsc[0:1, 2:3], nsc[0:1, 0:1],
                                                  nsc[0:1, 1:2], OP.mult),
              waits=[(B.sV, P.now(B.sV))], inc=B.sV)
    am = P.op("scalar", lambda e: e.activation(nsc[0:1, 3:4], nsc[0:1, 2:3], AF.Sqrt),
              waits=[(B.sV, vm)], inc=B.sA)
    vneg = P.op("vector", lambda e: e.tensor_scalar(nsc_bf[0:1, 0:1], nsc[0:1, 3:4],
                                                    -1.0 / 8.0, None, OP.mult),
                waits=[(B.sA, am)], inc=B.sV)
    mrep = P.op("tensor", lambda e: e.matmul(ps.N[:, 0:1], B.ones_bf[0:1, :],
                                             nsc_bf[0:1, 0:1], start=True, stop=True),
                waits=[(B.sV, vneg)], inc=B.sMM)
    vnegm = P.op("vector", lambda e: e.tensor_copy(negm[:, 0:1], ps.N[:, 0:1]),
                 waits=[(B.sMM, mrep)], inc=B.sV)
    if PROBE_NEGM[0]:
        vnegm = P.op("vector", lambda e: e.memset(negm[:, 0:1], -3.0),
                     waits=[(B.sV, vnegm)], inc=B.sV)

    bo_l = load_brow("bo")
    for hh in range(2):
        emit_replicate_row(P, B, brow[0:1, hh * 512:(hh + 1) * 512],
                           borep[:, hh * 512:(hh + 1) * 512], ps.N[:, 0:512], 512,
                           waits=[(B.sGD, bo_l), (B.sV, P.now(B.sV)), (B.sMM, P.now(B.sMM))])
    borep_v = P.now(B.sV)

    wo_ld = P.op("gpsimd", lambda e: e.dma_start(
        out=Wb.rearrange("p (i n) -> p i n", i=8),
        in_=wts["woT"].rearrange("(i p) n -> p i n", p=128)),
        waits=[(B.sMM, P.now(B.sMM))], inc=B.sGD, inc_n=16)

    st_fin = [0]

    def on_qb(qb, repack_gd):
        last_mm = 0
        for qt in range(2):
            g_rt = qb * 2 + qt
            xld = P.op("sync", lambda e, g_rt=g_rt: e.dma_start(
                out=xrow[:, 0:D], in_=x_own[g_rt * 128:(g_rt + 1) * 128, :]),
                waits=[(B.sV, P.now(B.sV)), (B.sGD, P.now(B.sGD))], inc=B.sLD, inc_n=16)
            for ncol in range(2):
                mf = None
                for j in range(8):
                    w = ([(B.sGD, repack_gd), (B.sGD, wo_ld), (B.sV, P.now(B.sV)),
                          (B.sA, P.now(B.sA))] if j == 0 else [])
                    mf = P.op("tensor", lambda e, j=j, qt=qt, ncol=ncol: e.matmul(
                        ps.T[:, 0:512],
                        ctx_pair[:, j * QB + qt * 128:j * QB + (qt + 1) * 128],
                        Wb[:, j * D + ncol * 512:j * D + (ncol + 1) * 512],
                        start=(j == 0), stop=(j == 7)),
                        waits=w, inc=B.sMM)
                last_mm = mf
                P.op("vector", lambda e, ncol=ncol, mf=mf: e.tensor_copy(
                    attn_sb[:, ncol * 512:(ncol + 1) * 512], ps.T[:, 0:512]),
                    waits=[(B.sMM, mf), (B.sGD, P.now(B.sGD)), (B.sV, borep_v)], inc=B.sV)
            v1 = P.op("vector", lambda e: e.tensor_tensor(
                attn_sb[:, 0:D], attn_sb[:, 0:D], borep[:, 0:D], OP.add),
                waits=[(B.sV, P.now(B.sV))], inc=B.sV)
            v2 = P.op("vector", lambda e: e.tensor_tensor(
                xrow[:, 0:D], xrow[:, 0:D], attn_sb[:, 0:D], OP.add),
                waits=[(B.sV, v1), (B.sLD, xld)], inc=B.sV)
            st_fin[0] = P.op("gpsimd", lambda e, g_rt=g_rt: e.dma_start(
                out=x_out[g_rt * 128:(g_rt + 1) * 128, :], in_=xrow[:, 0:D]),
                waits=[(B.sV, v2)], inc=B.sGD, inc_n=16)
        return last_mm

    if B3_DEBUG[0]:
        allw = [(s, P.now(s)) for s in P.tot]
        for nm, src_sb, shp, dt in (
                ("dbg_KT", KT, [128, 8 * S], BF16),
                ("dbg_VA", V_aug, [128, NK * VSTRIDE], BF16),
                ("dbg_nsc", nsc, [1, 8], F32),
                ("dbg_negm", negm, [128, 1], F32)):
            dd = nc.dram_tensor(nm, shp, dt, kind="ExternalOutput").ap()
            P.op("gpsimd", lambda e, dd=dd, src_sb=src_sb: e.dma_start(
                out=dd[:, :] if dd.shape[0] > 1 else dd[0:1, :], in_=src_sb[:, :]),
                waits=allw, inc=B.sGD, inc_n=16)
        dqt = nc.dram_tensor("dbg_qt", [128, 8 * R], BF16, kind="ExternalOutput").ap()
        P.op("gpsimd", lambda e: e.dma_start(out=dqt[:, :], in_=qt_dram[:, :]),
             waits=allw, inc=B.sGD, inc_n=16)
        dxc = nc.dram_tensor("dbg_xctx", [S, D], F32, kind="ExternalOutput").ap()
        P.op("gpsimd", lambda e: e.dma_start(out=dxc[:, :], in_=x_ctx[:, :]),
             waits=allw, inc=B.sGD, inc_n=16)
        dxo = nc.dram_tensor("dbg_xown", [R, D], F32, kind="ExternalOutput").ap()
        P.op("gpsimd", lambda e: e.dma_start(out=dxo[:, :], in_=x_own[:, :]),
             waits=allw, inc=B.sGD, inc_n=16)
        dtr = nc.dram_tensor("dbg_ntr", [1, 64], F32, kind="ExternalOutput").ap()
        P.op("gpsimd", lambda e: e.dma_start(out=dtr[0:1, :], in_=B.scr_c[0:1, 0:64]),
             waits=allw, inc=B.sGD, inc_n=16)
        dpn = nc.dram_tensor("dbg_psn00", [1, 512], F32, kind="ExternalOutput").ap()
        P.op("gpsimd", lambda e: e.dma_start(out=dpn[0:1, :], in_=B.greps[0:1, 0:512]),
             waits=allw, inc=B.sGD, inc_n=16)
    QTe = XTg[:, 0:8 * QB]
    QTo = XTg[:, 8 * QB:16 * QB]
    mha_attention(P, B, ps, KT, V_aug, qt_dram, QTe, QTo, E_sb, ctx64, ctx_pair, rep_sb,
                  recd64, negm, 1.0 / 8.0, (B.sV, vnegm), on_qb)
    P.op("gpsimd", lambda e: e.nop(), waits=[(B.sGD, P.now(B.sGD))])





def emit_L0_ffn(nc, P, B, ps, io):
    pref = io["pref"]
    x_own = io["x_own"]
    x_ctx = io["x_ctx"]
    x_out = io["x_out"]
    qt_dram = io["qt_dram"]
    xpp_dram = io["xpp_dram"]
    first = io.get("first", True)
    if first:
        wts_d = {n: nc.dram_tensor(pref + n, [D, D], F32, kind="ExternalInput").ap()
                 for n in ("wq", "wk", "wv", "wo")}
        Wgd_ = nc.dram_tensor(pref + "Wg", [D, 8], F32, kind="ExternalInput").ap()
        W1d_ = nc.dram_tensor(pref + "W1", [8, D, 4096], F32, kind="ExternalInput").ap()
        W2d_ = nc.dram_tensor(pref + "W2", [8, 4096, D], F32, kind="ExternalInput").ap()
        b1d_ = nc.dram_tensor(pref + "b1", [8, 4096], F32, kind="ExternalInput").ap()
        b2d_ = nc.dram_tensor(pref + "b2", [8, D], F32, kind="ExternalInput").ap()
        bias_d = {n: nc.dram_tensor(pref + n, [1, D], F32, kind="ExternalInput").ap()
                  for n in ("ln1_g", "ln1_b", "ln2_g", "ln2_b")}
        io["_dram"] = (wts_d, Wgd_, W1d_, W2d_, b1d_, b2d_, bias_d)
    wts, Wgd, W1d, W2d, b1d, b2d, bias = io["_dram"]
    KT = B.KT
    V_aug = B.V_aug
    xg = B.xg
    Wb = B.Wb
    xn_bf = B.xn_bf
    XTg = B.XTg
    brow = B.brow
    greps = B.greps
    nsc = B.nsc
    nsc_bf = B.nsc_bf
    negm = B.negm
    E_sb = B.E_sb
    sqn = B.E_sb[:, 0:512]
    ctx64 = B.ctx64
    ctx_pair = B.ctx_pair
    rep_sb = B.rep_sb
    recd64 = B.recd64
    attn_sb = B.attn_sb
    xrow = B.xrow
    b1col = B.b1col
    comb_sb = B.comb_sb
    qstage = B.qstage
    P.op("vector", lambda e: e.memset(nsc[0:1, 0:4], 0.0),
         waits=[(B.sA, P.now(B.sA)), (B.sMM, P.now(B.sMM))], inc=B.sV)

    def load_brow(name, waits=()):
        return P.op("gpsimd", lambda e: e.dma_start(out=brow[0:1, 0:D], in_=bias[name][:, :]),
                    waits=list(waits) + [(B.sMM, P.now(B.sMM))], inc=B.sGD, inc_n=16)

    for gi, n in enumerate(["ln1_g", "ln1_b"]):
        gld = load_brow(n)
        for hh in range(2):
            emit_replicate_row(P, B, brow[0:1, hh * 512:(hh + 1) * 512],
                               greps[:, gi * D + hh * 512:gi * D + (hh + 1) * 512],
                               ps.N[:, 0:512], 512,
                               waits=[(B.sGD, gld), (B.sLD, P.now(B.sLD)), (B.sV, P.now(B.sV))])
    grep_v = P.now(B.sV)

    va4 = V_aug.rearrange("p (k h m) -> p k h m", k=NK, h=NH)
    ones_v = P.op("vector", lambda e: e.memset(va4[:, :, :, DH:DH + 1], 1.0), inc=B.sV)

    def production(src_ap, n_rows, wname, dst):
        wld = P.op("gpsimd", lambda e: e.dma_start(
            out=Wb.rearrange("p (i n) -> p i n", i=8),
            in_=wts[wname].rearrange("(i p) n -> p i n", p=128)),
            waits=[(B.sMM, P.now(B.sMM))], inc=B.sGD, inc_n=16)
        n_groups = n_rows // 512
        xg_guard = [[]]
        qtdma_prev = [0]
        for g in range(n_groups):
            xt_v = 0
            for t in range(4):
                gt = g * 4 + t
                ld = P.op("sync", lambda e, gt=gt: e.dma_start(
                    out=xg[:, 0:D], in_=src_ap[gt * 128:(gt + 1) * 128, :]),
                    waits=xg_guard[0], inc=B.sLD, inc_n=16)
                aw, vL = emit_ln(P, B, xg[:, 0:D], B.scr_ln[:, 0:D],
                                 [(B.sLD, ld), (B.sV, grep_v)])
                aw2, vL2 = emit_ln(P, B, B.scr_ln[:, 0:D], xn_bf[:, 0:D],
                                   [aw], gain_rep=greps[:, 0:D], bias_rep=greps[:, D:2 * D])
                for i in range(8):
                    xt_v = emit_transpose(P, B, xn_bf[:, i * 128:(i + 1) * 128],
                                          XTg[:, i * 512 + t * 128:i * 512 + (t + 1) * 128],
                                          tr_slot(ps, t * 8 + i), waits=[(B.sV, vL2)])
                xg_guard[0] = [(B.sV, xt_v), (B.sA, P.now(B.sA))]
            if dst in ("KT", "QT"):
                for j in range(8):
                    bank = [ps.A, ps.B][j % 2]
                    mf = None
                    for i in range(8):
                        w = ([(B.sV, xt_v), (B.sGD, wld), (B.sV, P.now(B.sV))]
                             if i == 0 else [])
                        mf = P.op("tensor", lambda e, i=i, j=j, bank=bank: e.matmul(
                            bank[:, 0:512],
                            Wb[:, i * D + j * 128:i * D + (j + 1) * 128],
                            XTg[:, i * 512:(i + 1) * 512],
                            start=(i == 0), stop=(i == 7)),
                            waits=w, inc=B.sMM)
                    if dst == "KT":
                        tgt = KT[:, j * S + g * 512:j * S + (g + 1) * 512]
                        ev = P.op("vector", lambda e, tgt=tgt, bank=bank: e.tensor_copy(
                            tgt, bank[:, 0:512]), waits=[(B.sMM, mf)], inc=B.sV)
                        ncol_sel = 1
                    else:
                        jp = j % 2
                        tgt = qstage[:, jp * 512:(jp + 1) * 512]
                        ev = P.op("vector", lambda e, tgt=tgt, bank=bank: e.tensor_copy(
                            tgt, bank[:, 0:512]),
                            waits=[(B.sMM, mf), (B.sLD, qtdma_prev[0])], inc=B.sV)
                        qtdma_prev[0] = P.op("sync", lambda e, j=j, g=g, tgt=tgt: e.dma_start(
                            out=qt_dram[:, j * R + g * 512:j * R + (g + 1) * 512], in_=tgt),
                            waits=[(B.sV, ev)], inc=B.sLD, inc_n=16)
                        ncol_sel = 0
                    sq = P.op("vector", lambda e, tgt=tgt: e.tensor_tensor(
                        sqn[:, 0:512], tgt, tgt, OP.mult),
                        waits=[(B.sV, P.now(B.sV)), (B.sMM, P.now(B.sMM))], inc=B.sV)
                    nmm = P.op("tensor", lambda e: e.matmul(
                        ps.N[0:1, 0:512], B.ones_bf[:, 0:1], sqn[:, 0:512],
                        start=True, stop=True),
                        waits=[(B.sV, sq)], inc=B.sMM)
                    rv_ = P.op("vector", lambda e: e.reduce_max(nsc[0:1, 2:3], ps.N[0:1, 0:512],
                                                          axis=AX.X),
                         waits=[(B.sMM, nmm)], inc=B.sV)
                    P.op("vector", lambda e, ncol_sel=ncol_sel: e.tensor_tensor(
                        nsc[0:1, ncol_sel:ncol_sel + 1], nsc[0:1, ncol_sel:ncol_sel + 1],
                        nsc[0:1, 2:3], OP.max), waits=[(B.sV, rv_)], inc=B.sV)
            else:
                for rt in range(4):
                    g_rt = g * 4 + rt
                    bank = [ps.A, ps.B][rt % 2]
                    for ncol in range(2):
                        mf = None
                        for i in range(8):
                            w = ([(B.sV, xt_v), (B.sGD, wld), (B.sV, P.now(B.sV))]
                                 if i == 0 else [])
                            mf = P.op("tensor", lambda e, i=i, rt=rt, ncol=ncol, bank=bank: e.matmul(
                                bank[:, 0:512],
                                XTg[:, i * 512 + rt * 128:i * 512 + (rt + 1) * 128],
                                Wb[:, i * D + ncol * 512:i * D + (ncol + 1) * 512],
                                start=(i == 0), stop=(i == 7)),
                                waits=w, inc=B.sMM)
                        dst_ap = va4[:, g_rt:g_rt + 1, ncol * 8:(ncol + 1) * 8, 0:DH]
                        P.op("vector", lambda e, dst_ap=dst_ap, bank=bank: e.tensor_copy(
                            dst_ap, bank[:, 0:512].rearrange("p (o h m) -> p o h m", o=1, h=8)),
                            waits=[(B.sMM, mf), (B.sV, ones_v)], inc=B.sV)

    production(x_ctx, S, "wk", "KT")
    production(x_ctx, S, "wv", "V")
    production(x_own, R, "wq", "QT")

    vm = P.op("vector", lambda e: e.tensor_tensor(nsc[0:1, 2:3], nsc[0:1, 0:1],
                                                  nsc[0:1, 1:2], OP.mult),
              waits=[(B.sV, P.now(B.sV))], inc=B.sV)
    am = P.op("scalar", lambda e: e.activation(nsc[0:1, 3:4], nsc[0:1, 2:3], AF.Sqrt),
              waits=[(B.sV, vm)], inc=B.sA)
    vneg = P.op("vector", lambda e: e.tensor_scalar(nsc_bf[0:1, 0:1], nsc[0:1, 3:4],
                                                    -1.0 / 8.0, None, OP.mult),
                waits=[(B.sA, am)], inc=B.sV)
    mrep = P.op("tensor", lambda e: e.matmul(ps.N[:, 0:1], B.ones_bf[0:1, :],
                                             nsc_bf[0:1, 0:1], start=True, stop=True),
                waits=[(B.sV, vneg)], inc=B.sMM)
    vnegm = P.op("vector", lambda e: e.tensor_copy(negm[:, 0:1], ps.N[:, 0:1]),
                 waits=[(B.sMM, mrep)], inc=B.sV)

    wo_ld = P.op("gpsimd", lambda e: e.dma_start(
        out=Wb.rearrange("p (i n) -> p i n", i=8),
        in_=wts["wo"].rearrange("(i p) n -> p i n", p=128)),
        waits=[(B.sMM, P.now(B.sMM))], inc=B.sGD, inc_n=16)

    def on_qb(qb, repack_gd):
        last_mm = 0
        for qt in range(2):
            g_rt = qb * 2 + qt
            xld = P.op("sync", lambda e, g_rt=g_rt: e.dma_start(
                out=xrow[:, 0:D], in_=x_own[g_rt * 128:(g_rt + 1) * 128, :]),
                waits=[(B.sV, P.now(B.sV)), (B.sGD, P.now(B.sGD))], inc=B.sLD, inc_n=16)
            for ncol in range(2):
                mf = None
                for j in range(8):
                    w = ([(B.sGD, repack_gd), (B.sGD, wo_ld), (B.sV, P.now(B.sV)),
                          (B.sA, P.now(B.sA))] if j == 0 else [])
                    mf = P.op("tensor", lambda e, j=j, qt=qt, ncol=ncol: e.matmul(
                        ps.T[:, 0:512],
                        ctx_pair[:, j * QB + qt * 128:j * QB + (qt + 1) * 128],
                        Wb[:, j * D + ncol * 512:j * D + (ncol + 1) * 512],
                        start=(j == 0), stop=(j == 7)),
                        waits=w, inc=B.sMM)
                last_mm = mf
                P.op("vector", lambda e, ncol=ncol, mf=mf: e.tensor_copy(
                    attn_sb[:, ncol * 512:(ncol + 1) * 512], ps.T[:, 0:512]),
                    waits=[(B.sMM, mf), (B.sGD, P.now(B.sGD))], inc=B.sV)
            aw, vL = emit_ln(P, B, xrow[:, 0:D], B.scr_ln[:, 0:D],
                             [(B.sLD, xld), (B.sV, P.now(B.sV))])
            v1 = P.op("vector", lambda e: e.tensor_tensor(
                attn_sb[:, 0:D], attn_sb[:, 0:D], B.scr_ln[:, 0:D], OP.add),
                waits=[aw, (B.sV, vL)], inc=B.sV)
            v2 = P.op("vector", lambda e: e.tensor_tensor(
                xrow[:, 0:D], xrow[:, 0:D], attn_sb[:, 0:D], OP.add),
                waits=[(B.sV, v1)], inc=B.sV)
            P.op("gpsimd", lambda e, g_rt=g_rt: e.dma_start(
                out=x_out[g_rt * 128:(g_rt + 1) * 128, :], in_=xrow[:, 0:D]),
                waits=[(B.sV, v2)], inc=B.sGD, inc_n=16)
            P.op("gpsimd", lambda e, g_rt=g_rt: e.dma_start(
                out=xpp_dram[g_rt * 128:(g_rt + 1) * 128, :], in_=attn_sb[:, 0:D]),
                waits=[(B.sV, v2)], inc=B.sGD, inc_n=16)
        return last_mm

    QTe = XTg[:, 0:8 * QB]
    QTo = XTg[:, 8 * QB:16 * QB]
    mha_attention(P, B, ps, KT, V_aug, qt_dram, QTe, QTo, E_sb, ctx64, ctx_pair, rep_sb,
                  recd64, negm, 1.0 / 8.0, (B.sV, vnegm), on_qb)

    for gi, n in enumerate(["ln2_g", "ln2_b"]):
        gld = load_brow(n, waits=[(B.sV, P.now(B.sV))])
        for hh in range(2):
            emit_replicate_row(P, B, brow[0:1, hh * 512:(hh + 1) * 512],
                               greps[:, gi * D + hh * 512:gi * D + (hh + 1) * 512],
                               ps.N[:, 0:512], 512,
                               waits=[(B.sGD, gld), (B.sV, P.now(B.sV)), (B.sMM, P.now(B.sMM))])
    Wg_sb = B.gw16[:, 0:64]
    wgld = P.op("gpsimd", lambda e: e.dma_start(
        out=Wg_sb.rearrange("p (i n) -> p i n", i=8),
        in_=Wgd.rearrange("(i p) n -> p i n", p=128)),
        waits=[(B.sMM, P.now(B.sMM))], inc=B.sGD, inc_n=16)
    XT_m = KT[:, 0:8 * R]
    xgg = [[(B.sV, P.now(B.sV)), (B.sA, P.now(B.sA)), (B.sGD, P.now(B.sGD)),
            (B.sMM, P.now(B.sMM))]]
    for gt in range(R // 128):
        ldx = P.op("sync", lambda e, gt=gt: e.dma_start(
            out=xg[:, 0:D], in_=xpp_dram[gt * 128:(gt + 1) * 128, :]),
            waits=xgg[0], inc=B.sLD, inc_n=16)
        aw2, vL2 = emit_ln(P, B, xg[:, 0:D], xn_bf[:, 0:D],
                           [(B.sLD, ldx)], gain_rep=greps[:, 0:D],
                           bias_rep=greps[:, D:2 * D])
        xt_v = 0
        for i in range(8):
            xt_v = emit_transpose(P, B, xn_bf[:, i * 128:(i + 1) * 128],
                                  XT_m[:, i * R + gt * 128:i * R + (gt + 1) * 128],
                                  tr_slot(ps, i), waits=[(B.sV, vL2)])
        xgg[0] = [(B.sV, xt_v), (B.sA, P.now(B.sA))]
    xtm_v = P.now(B.sV)

    st_g = B.stats
    for rt in range(R // 128):
        mf = None
        for i in range(8):
            w = ([(B.sV, xtm_v), (B.sGD, wgld), (B.sV, P.now(B.sV)),
                  (B.sA, P.now(B.sA))] if i == 0 else [])
            mf = P.op("tensor", lambda e, i=i, rt=rt: e.matmul(
                ps.N[:, 0:8],
                XT_m[:, i * R + rt * 128:i * R + (rt + 1) * 128],
                Wg_sb[:, i * 8:(i + 1) * 8],
                start=(i == 0), stop=(i == 7)),
                waits=w, inc=B.sMM)
        v1 = P.op("vector", lambda e: e.reduce_max(st_g[:, 0:1], ps.N[:, 0:8], axis=AX.X),
                  waits=[(B.sMM, mf)], inc=B.sV)
        v2 = P.op("vector", lambda e: e.tensor_scalar(st_g[:, 1:2], st_g[:, 0:1], -1.0,
                                                      None, OP.mult),
                  waits=[(B.sV, v1)], inc=B.sV)
        a1 = P.op("scalar", lambda e: e.activation(B.scr_ln[:, 0:8], ps.N[:, 0:8], AF.Exp,
                                                   bias=st_g[:, 1:2]),
                  waits=[(B.sV, v2)], inc=B.sA)
        v3 = P.op("vector", lambda e: e.reduce_sum(st_g[:, 2:3], B.scr_ln[:, 0:8], axis=AX.X),
                  waits=[(B.sA, a1)], inc=B.sV)
        v4 = P.op("vector", lambda e: e.reciprocal(st_g[:, 3:4], st_g[:, 2:3]),
                  waits=[(B.sV, v3)], inc=B.sV)
        v5 = P.op("vector", lambda e: e.tensor_scalar(B.scr_ln[:, 0:8], B.scr_ln[:, 0:8],
                                                      st_g[:, 3:4], None, OP.mult),
                  waits=[(B.sV, v4)], inc=B.sV)
        v6 = P.op("vector", lambda e: e.reduce_max(st_g[:, 4:5], B.scr_ln[:, 0:8], axis=AX.X),
                  waits=[(B.sV, v5)], inc=B.sV)
        v7 = P.op("vector", lambda e: e.tensor_scalar(B.scr_ln[:, 8:16], B.scr_ln[:, 0:8],
                                                      st_g[:, 4:5], None, OP.is_ge),
                  waits=[(B.sV, v6)], inc=B.sV)
        v8 = P.op("vector", lambda e: e.scalar_tensor_tensor(B.scr_ln[:, 16:24],
                                                             B.scr_ln[:, 8:16], -1e30,
                                                             B.scr_ln[:, 0:8],
                                                             OP.mult, OP.add),
                  waits=[(B.sV, v7)], inc=B.sV)
        v9 = P.op("vector", lambda e: e.reduce_max(st_g[:, 5:6], B.scr_ln[:, 16:24], axis=AX.X),
                  waits=[(B.sV, v8)], inc=B.sV)
        v10 = P.op("vector", lambda e: e.tensor_scalar(B.scr_ln[:, 16:24], B.scr_ln[:, 16:24],
                                                       st_g[:, 5:6], None, OP.is_ge),
                   waits=[(B.sV, v9)], inc=B.sV)
        v11 = P.op("vector", lambda e: e.tensor_tensor(st_g[:, 6:7], st_g[:, 4:5],
                                                       st_g[:, 5:6], OP.add),
                   waits=[(B.sV, v10)], inc=B.sV)
        v12 = P.op("vector", lambda e: e.reciprocal(st_g[:, 7:8], st_g[:, 6:7]),
                   waits=[(B.sV, v11)], inc=B.sV)
        v13 = P.op("vector", lambda e: e.tensor_scalar(B.scr_ln[:, 8:16], B.scr_ln[:, 8:16],
                                                       st_g[:, 4:5], None, OP.mult),
                   waits=[(B.sV, v12)], inc=B.sV)
        v14 = P.op("vector", lambda e: e.tensor_scalar(B.scr_ln[:, 16:24], B.scr_ln[:, 16:24],
                                                       st_g[:, 5:6], None, OP.mult),
                   waits=[(B.sV, v13)], inc=B.sV)
        v15 = P.op("vector", lambda e: e.tensor_tensor(B.scr_ln[:, 8:16], B.scr_ln[:, 8:16],
                                                       B.scr_ln[:, 16:24], OP.add),
                   waits=[(B.sV, v14)], inc=B.sV)
        P.op("vector", lambda e, rt=rt: e.tensor_scalar(comb_sb[:, rt * 8:(rt + 1) * 8],
                                                        B.scr_ln[:, 8:16], st_g[:, 7:8],
                                                        None, OP.mult),
             waits=[(B.sV, v15)], inc=B.sV)
    comb_v = P.now(B.sV)

    W1s = V_aug[:, 0:8 * 2048]
    W2s = V_aug[:, 8 * 2048:8 * 2048 + 16 * D]
    h_bf = XTg[:, 0:16 * QB]
    u_scr = B.scr_b[:, 0:QB]
    w_scr = B.scr_b[:, QB:2 * QB]
    a_scr = B.scr_b[:, 2 * QB:3 * QB]
    sg_scr = B.scr_b[:, 3 * QB:4 * QB]
    acc_guard = {}
    base_v = P.now(B.sV)
    bank_g1 = [base_v, base_v]
    cb_free = [base_v] * 4
    vbufs = [B.scr_c[:, 0:512], B.attn_sb[:, 0:512]]
    vbuf_gd = [P.now(B.sGD), P.now(B.sGD)]
    vk = [0]
    for e_i in range(8):
        for fs in range(2):
            w1ld = P.op("gpsimd", lambda e, e_i=e_i, fs=fs: e.dma_start(
                out=W1s.rearrange("p (i f) -> p i f", i=8),
                in_=W1d[e_i, :, fs * 2048:(fs + 1) * 2048].rearrange("(i p) f -> p i f", p=128)),
                waits=[(B.sMM, P.now(B.sMM))], inc=B.sGD, inc_n=16)
            w2ld = P.op("gpsimd", lambda e, e_i=e_i, fs=fs: e.dma_start(
                out=W2s.rearrange("p (i f) -> p i f", i=16),
                in_=W2d[e_i, fs * 2048:(fs + 1) * 2048, :].rearrange("(i p) f -> p i f", p=128)),
                inc=B.sGD, inc_n=16)
            bcld = P.op("gpsimd", lambda e, e_i=e_i, fs=fs: e.dma_start(
                out=b1col[:, 0:16].rearrange("p (o t) -> p o t", o=1),
                in_=b1d[e_i:e_i + 1, fs * 2048:(fs + 1) * 2048].rearrange("o (t p) -> p o t", p=128),
                allow_slow_non_contiguous=True),
                waits=[(B.sV, P.now(B.sV))], inc=B.sGD, inc_n=16)
            b2ld = P.op("gpsimd", lambda e, e_i=e_i: e.dma_start(
                out=brow[0:1, 0:D], in_=b2d[e_i:e_i + 1, :]),
                waits=[(B.sMM, P.now(B.sMM))], inc=B.sGD, inc_n=16) if fs == 1 else 0
            for rc in range(8):
                for fm in range(16):
                    bank = [ps.A, ps.B][fm % 2]
                    mf = None
                    for i in range(8):
                        w = ([(B.sGD, max(w1ld, bcld)), (B.sV, bank_g1[fm % 2]),
                              (B.sV, comb_v)] if i == 0 else [])
                        mf = P.op("tensor", lambda e, i=i, fm=fm, rc=rc, bank=bank: e.matmul(
                            bank[:, 0:QB],
                            W1s[:, i * 2048 + fm * 128:i * 2048 + (fm + 1) * 128],
                            XT_m[:, i * R + rc * QB:i * R + (rc + 1) * QB],
                            start=(i == 0), stop=(i == 7)),
                            waits=w, inc=B.sMM)
                    g1 = P.op("vector", lambda e, fm=fm, bank=bank: e.tensor_scalar(
                        u_scr, bank[:, 0:QB], b1col[:, fm:fm + 1], None, OP.add),
                        waits=[(B.sMM, mf)], inc=B.sV)
                    bank_g1[fm % 2] = g1
                    g2 = P.op("vector", lambda e: e.tensor_tensor(w_scr, u_scr, u_scr, OP.mult),
                              waits=[(B.sV, g1)], inc=B.sV)
                    g3 = P.op("vector", lambda e: e.scalar_tensor_tensor(
                        a_scr, w_scr, GELU_B, u_scr, OP.add, OP.mult),
                        waits=[(B.sV, g2)], inc=B.sV)
                    g4 = P.op("scalar", lambda e: e.activation(sg_scr, a_scr, AF.Sigmoid,
                                                               scale=GELU_A),
                              waits=[(B.sV, g3)], inc=B.sA)
                    g5 = P.op("vector", lambda e, fm=fm: e.tensor_tensor(
                        h_bf[:, fm * QB:(fm + 1) * QB], sg_scr, u_scr, OP.mult),
                        waits=[(B.sA, g4)], inc=B.sV)
                h_v = P.now(B.sV)
                for rt in range(2):
                    for dcol in range(2):
                        cb = ps.C[rt * 2 + dcol]
                        mf = None
                        for i in range(16):
                            w = ([(B.sV, h_v), (B.sGD, w2ld),
                                  (B.sV, cb_free[rt * 2 + dcol])]
                                 if i == 0 else [])
                            mf = P.op("tensor", lambda e, i=i, rt=rt, dcol=dcol, cb=cb: e.matmul(
                                cb[:, 0:512],
                                h_bf[:, i * QB + rt * 128:i * QB + (rt + 1) * 128],
                                W2s[:, i * D + dcol * 512:i * D + (dcol + 1) * 512],
                                start=(i == 0), stop=(i == 15)),
                                waits=w, inc=B.sMM)
                        g_rt = rc * 2 + rt
                        if fs == 1:
                            mf = P.op("tensor", lambda e, dcol=dcol, cb=cb: e.matmul(
                                cb[:, 0:512], B.ones_bf[0:1, :],
                                brow[0:1, dcol * 512:(dcol + 1) * 512],
                                start=False, stop=True),
                                waits=[(B.sGD, b2ld)], inc=B.sMM)
                        vb = vk[0] % 2
                        vk[0] += 1
                        vt = P.op("vector", lambda e, g_rt=g_rt, e_i=e_i, cb=cb, vb=vb: e.tensor_scalar(
                            vbufs[vb], cb[:, 0:512],
                            comb_sb[:, g_rt * 8 + e_i:g_rt * 8 + e_i + 1], None, OP.mult),
                            waits=[(B.sMM, mf), (B.sGD, vbuf_gd[vb])],
                            inc=B.sV)
                        cb_free[rt * 2 + dcol] = vt
                        vbuf_gd[vb] = P.op("gpsimd", lambda e, g_rt=g_rt, dcol=dcol, vb=vb: e.dma_start(
                            out=x_out[g_rt * 128:(g_rt + 1) * 128, dcol * 512:(dcol + 1) * 512],
                            in_=vbufs[vb], accum_op=OP.add),
                            waits=[(B.sV, vt)], inc=B.sGD, inc_n=16)
    P.op("gpsimd", lambda e: e.nop(), waits=[(B.sGD, P.now(B.sGD))])





def emit_L1_ffn(nc, P, B, ps, io):
    pref = io["pref"]
    x_own = io["x_own"]
    x_out = io["x_out"]
    first = io.get("first", True)
    if first:
        io["_dram"] = (
            nc.dram_tensor(pref + "gate_wT", [D, 16], F32, kind="ExternalInput").ap(),
            nc.dram_tensor(pref + "gate_b", [1, 16], F32, kind="ExternalInput").ap(),
            nc.dram_tensor(pref + "w13", [16, D, D], F32, kind="ExternalInput").ap(),
            nc.dram_tensor(pref + "w2", [16, D, 512], F32, kind="ExternalInput").ap(),
            nc.dram_tensor(pref + "w13_s", [D, D], F32, kind="ExternalInput").ap(),
            nc.dram_tensor(pref + "w2_s", [D, 512], F32, kind="ExternalInput").ap(),
        )
    gwT, gbd, W13d, W2dd, W13s, W2s_d = io["_dram"]
    XT_m = B.XT_m16
    w13_raw = B.w13_raw
    w13T = B.w13T
    w2_raw = B.w2_raw
    w2T = B.w2T
    gw_sb = B.gw16
    gb_bf = B.gb_bf
    gb_rep = B.gb_rep
    xg = B.xg
    xn_bf = B.xn_bf
    act_bf = B.act_bf
    comb_sb = B.comb_sb
    gwld = P.op("gpsimd", lambda e: e.dma_start(
        out=gw_sb.rearrange("p (i n) -> p i n", i=8),
        in_=gwT.rearrange("(i p) n -> p i n", p=128)), inc=B.sGD, inc_n=16)
    gbld = P.op("gpsimd", lambda e: e.dma_start(out=gb_bf[0:1, :], in_=gbd[:, :]),
                inc=B.sGD, inc_n=16)
    emit_replicate_row(P, B, gb_bf[0:1, 0:16], gb_rep[:, 0:16], ps.N[:, 0:16], 16,
                       waits=[(B.sGD, gbld), (B.sLD, P.now(B.sLD))])
    gb_v = P.now(B.sV)

    xgg = [[]]
    st_t = 0
    for gt in range(R // 128):
        ld = P.op("sync", lambda e, gt=gt: e.dma_start(
            out=xg[:, 0:D], in_=x_own[gt * 128:(gt + 1) * 128, :]),
            waits=xgg[0], inc=B.sLD, inc_n=16)
        aw, vL = emit_ln(P, B, xg[:, 0:D], xn_bf[:, 0:D], [(B.sLD, ld)])
        xt_v = 0
        for i in range(8):
            xt_v = emit_transpose(P, B, xn_bf[:, i * 128:(i + 1) * 128],
                                  XT_m[:, i * R + gt * 128:i * R + (gt + 1) * 128],
                                  tr_slot(ps, i), waits=[(B.sV, vL)])
        st_t = P.op("gpsimd", lambda e, gt=gt: e.dma_start(
            out=x_out[gt * 128:(gt + 1) * 128, :], in_=xg[:, 0:D]),
            waits=[(B.sV, xt_v), (B.sA, P.now(B.sA))], inc=B.sGD, inc_n=16)
        xgg[0] = [(B.sV, xt_v), (B.sA, P.now(B.sA)), (B.sGD, st_t)]
    xtm_v = P.now(B.sV)

    st_g = B.stats
    for rt in range(R // 128):
        mf = None
        for i in range(8):
            w = ([(B.sV, xtm_v), (B.sGD, gwld), (B.sV, P.now(B.sV)),
                  (B.sA, P.now(B.sA))] if i == 0 else [])
            mf = P.op("tensor", lambda e, i=i, rt=rt: e.matmul(
                ps.N[:, 0:16],
                XT_m[:, i * R + rt * 128:i * R + (rt + 1) * 128],
                gw_sb[:, i * 16:(i + 1) * 16],
                start=(i == 0), stop=(i == 7)),
                waits=w, inc=B.sMM)
        a1 = P.op("scalar", lambda e: e.activation(B.scr_ln[:, 0:16], ps.N[:, 0:16],
                                                   AF.Sigmoid),
                  waits=[(B.sMM, mf), (B.sV, P.now(B.sV))], inc=B.sA)
        v1 = P.op("vector", lambda e: e.tensor_tensor(B.scr_ln[:, 16:32], B.scr_ln[:, 0:16],
                                                      gb_rep[:, 0:16], OP.add),
                  waits=[(B.sA, a1), (B.sV, gb_v)], inc=B.sV)
        v2 = P.op("vector", lambda e: e.memset(B.scr_ln[:, 32:48], 0.0),
                  waits=[(B.sV, v1)], inc=B.sV)
        v3 = P.op("vector", lambda e: e.memset(st_g[:, 8:9], 0.0),
                  waits=[(B.sV, v2)], inc=B.sV)
        vv = v3
        for j in range(4):
            m1 = P.op("vector", lambda e: e.reduce_max(st_g[:, 0:1], B.scr_ln[:, 16:32],
                                                       axis=AX.X),
                      waits=[(B.sV, vv)], inc=B.sV)
            mk = P.op("vector", lambda e: e.tensor_scalar(B.scr_ln[:, 48:64],
                                                          B.scr_ln[:, 16:32],
                                                          st_g[:, 0:1], None, OP.is_ge),
                      waits=[(B.sV, m1)], inc=B.sV)
            w1 = P.op("vector", lambda e: e.tensor_tensor(B.scr_ln[:, 64:80],
                                                          B.scr_ln[:, 48:64],
                                                          B.scr_ln[:, 0:16], OP.mult),
                      waits=[(B.sV, mk)], inc=B.sV)
            w2 = P.op("vector", lambda e: e.reduce_sum(st_g[:, 1:2], B.scr_ln[:, 64:80],
                                                       axis=AX.X),
                      waits=[(B.sV, w1)], inc=B.sV)
            w3 = P.op("vector", lambda e: e.tensor_tensor(st_g[:, 8:9], st_g[:, 8:9],
                                                          st_g[:, 1:2], OP.add),
                      waits=[(B.sV, w2)], inc=B.sV)
            i1 = P.op("vector", lambda e: e.tensor_tensor(B.scr_ln[:, 32:48],
                                                          B.scr_ln[:, 32:48],
                                                          B.scr_ln[:, 48:64], OP.add),
                      waits=[(B.sV, w3)], inc=B.sV)
            vv = P.op("vector", lambda e: e.scalar_tensor_tensor(B.scr_ln[:, 16:32],
                                                                 B.scr_ln[:, 48:64], -1e30,
                                                                 B.scr_ln[:, 16:32],
                                                                 OP.mult, OP.add),
                      waits=[(B.sV, i1)], inc=B.sV)
        r1 = P.op("vector", lambda e: e.reciprocal(st_g[:, 2:3], st_g[:, 8:9]),
                  waits=[(B.sV, vv)], inc=B.sV)
        r2 = P.op("vector", lambda e: e.tensor_scalar(st_g[:, 3:4], st_g[:, 2:3], 2.5,
                                                      None, OP.mult),
                  waits=[(B.sV, r1)], inc=B.sV)
        r3 = P.op("vector", lambda e: e.tensor_tensor(B.scr_ln[:, 32:48], B.scr_ln[:, 32:48],
                                                      B.scr_ln[:, 0:16], OP.mult),
                  waits=[(B.sV, r2)], inc=B.sV)
        P.op("vector", lambda e, rt=rt: e.tensor_scalar(comb_sb[:, rt * 16:(rt + 1) * 16],
                                                        B.scr_ln[:, 32:48], st_g[:, 3:4],
                                                        None, OP.mult),
             waits=[(B.sV, r3)], inc=B.sV)
    comb_v = P.now(B.sV)

    def expert(e_i, shared):
        if shared:
            w13src = W13s
            w2src = W2s_d
        else:
            w13src = W13d[e_i]
            w2src = W2dd[e_i]
        l1 = P.op("gpsimd", lambda e: e.dma_start(
            out=w13_raw.rearrange("p (i n) -> p i n", i=8),
            in_=w13src.rearrange("(i p) n -> p i n", p=128)),
            waits=[(B.sMM, P.now(B.sMM))], inc=B.sGD, inc_n=16)
        l2 = P.op("gpsimd", lambda e: e.dma_start(
            out=w2_raw.rearrange("p (i n) -> p i n", i=8),
            in_=w2src.rearrange("(i p) n -> p i n", p=128)),
            inc=B.sGD, inc_n=16)
        tv = 0
        for a in range(8):
            for b in range(8):
                tv = emit_transpose(P, B, w13_raw[:, a * D + b * 128:a * D + (b + 1) * 128],
                                    w13T[:, b * D + a * 128:b * D + (a + 1) * 128],
                                    tr_slot(ps, (a * 8 + b)),
                                    waits=[(B.sGD, l1), (B.sMM, P.now(B.sMM))])
        for a in range(8):
            for b in range(4):
                tv = emit_transpose(P, B, w2_raw[:, a * 512 + b * 128:a * 512 + (b + 1) * 128],
                                    w2T[:, b * D + a * 128:b * D + (a + 1) * 128],
                                    tr_slot(ps, (a * 4 + b)),
                                    waits=[(B.sGD, l2), (B.sMM, P.now(B.sMM))])
        lastA_sg = [P.now(B.sA)]
        lastB_vm = [P.now(B.sV)]
        cb_free = [P.now(B.sV)] * 4
        for rc in range(8):
            for mp in range(4):
                for half in range(2):
                    m = mp + 4 * half
                    bank = [ps.A, ps.B][half]
                    bank_w = ([(B.sA, lastA_sg[0])] if half == 0
                              else [(B.sV, lastB_vm[0])])
                    mf = None
                    for i in range(8):
                        w = ([(B.sV, tv), (B.sV, comb_v)] + bank_w if i == 0 else [])
                        mf = P.op("tensor", lambda e, i=i, m=m, rc=rc, bank=bank: e.matmul(
                            bank[:, 0:QB],
                            w13T[:, i * D + m * 128:i * D + (m + 1) * 128],
                            XT_m[:, i * R + rc * QB:i * R + (rc + 1) * QB],
                            start=(i == 0), stop=(i == 7)),
                            waits=w, inc=B.sMM)
                sg = P.op("scalar", lambda e: e.activation(B.scr_b[:, 0:QB], ps.A[:, 0:QB],
                                                           AF.Silu),
                          waits=[(B.sMM, P.now(B.sMM)), (B.sV, lastB_vm[0])], inc=B.sA)
                lastA_sg[0] = sg
                lastB_vm[0] = P.op("vector", lambda e, mp=mp: e.tensor_tensor(
                    act_bf[:, mp * QB:(mp + 1) * QB], ps.B[:, 0:QB], B.scr_b[:, 0:QB],
                    OP.mult),
                    waits=[(B.sA, sg)], inc=B.sV)
            h_v = P.now(B.sV)
            for rt in range(2):
                for dcol in range(2):
                    cb = ps.C[rt * 2 + dcol]
                    mf = None
                    for i in range(4):
                        w = ([(B.sV, h_v), (B.sV, cb_free[rt * 2 + dcol])]
                             if i == 0 else [])
                        mf = P.op("tensor", lambda e, i=i, rt=rt, dcol=dcol, cb=cb: e.matmul(
                            cb[:, 0:512],
                            act_bf[:, i * QB + rt * 128:i * QB + (rt + 1) * 128],
                            w2T[:, i * D + dcol * 512:i * D + (dcol + 1) * 512],
                            start=(i == 0), stop=(i == 3)),
                            waits=w, inc=B.sMM)
                    g_rt = rc * 2 + rt
                    vb = vk6[0] % 2
                    vk6[0] += 1
                    if shared:
                        vt = P.op("vector", lambda e, cb=cb, vb=vb: e.tensor_copy(
                            vbufs6[vb], cb[:, 0:512]),
                            waits=[(B.sMM, mf), (B.sGD, vbuf_gd6[vb])], inc=B.sV)
                    else:
                        vt = P.op("vector", lambda e, g_rt=g_rt, e_i=e_i, cb=cb, vb=vb: e.tensor_scalar(
                            vbufs6[vb], cb[:, 0:512],
                            comb_sb[:, g_rt * 16 + e_i:g_rt * 16 + e_i + 1], None, OP.mult),
                            waits=[(B.sMM, mf), (B.sGD, vbuf_gd6[vb])], inc=B.sV)
                    cb_free[rt * 2 + dcol] = vt
                    vbuf_gd6[vb] = P.op("gpsimd", lambda e, g_rt=g_rt, dcol=dcol, vb=vb: e.dma_start(
                        out=x_out[g_rt * 128:(g_rt + 1) * 128, dcol * 512:(dcol + 1) * 512],
                        in_=vbufs6[vb], accum_op=OP.add),
                        waits=[(B.sV, vt)], inc=B.sGD, inc_n=16)

    vbufs6 = [B.scr_c[:, 0:512], B.attn_sb[:, 0:512]]
    vbuf_gd6 = [P.now(B.sGD), P.now(B.sGD)]
    vk6 = [0]
    for e_i in range(16):
        expert(e_i, False)
    expert(0, True)
    P.op("gpsimd", lambda e: e.nop(), waits=[(B.sGD, P.now(B.sGD))])




def emit_L0_attn_a(nc, P, B, ps, io):
    pref = io["pref"]
    x_own = io["x_own"]
    x_prev = io["x_prev"]
    x_out = io["x_out"]
    wts = {n: nc.dram_tensor(pref + n, [D, D], F32, kind="ExternalInput").ap()
           for n in ("wqT", "wkT", "wvT")}
    bias = {n: nc.dram_tensor(pref + n, [1, D], F32, kind="ExternalInput").ap()
            for n in ("bq", "bk", "bv")}
    gam_d = nc.dram_tensor(pref + "gamma", [1, 1], F32, kind="ExternalInput").ap()
    valid_d = nc.dram_tensor(pref + "valid", [128, 1], F32, kind="ExternalInput").ap()
    mcl_d = nc.dram_tensor(pref + "mclampT", [128, 128], F32, kind="ExternalInput").ap()
    msk_d = nc.dram_tensor(pref + "maskT", [128, 128], F32, kind="ExternalInput").ap()
    cpos_d = nc.dram_tensor(pref + "cpos", [1, 128], F32, kind="ExternalInput").ap()
    crev_d = nc.dram_tensor(pref + "crev", [1, 128], F32, kind="ExternalInput").ap()
    S_sb = B.S_sb[:, 0:8 * D]
    S_bf = B.S_bf
    z_sb = B.z_sb
    z_bf = B.z_bf
    QT_g = B.QT_g
    KT_g = B.KT_g
    V_g = B.V_g
    krow = B.krow
    kd_bf = B.kd_bf
    xg = B.xg
    Wb = B.Wb
    xn_bf = B.xn_bf
    XTg = B.XTg
    A_T = B.A_T
    msk = B.msk
    mcl = B.mcl
    pp_bf = B.pp_bf
    cross_scr = B.cross_scr
    scr2 = B.scr2
    xrow = B.xrow
    bcol = B.b1col[:, 0:8]
    brow = B.brow
    gcols = B.gcols
    grow = B.grow
    growp = B.growp
    g_bf = B.g_bf
    lg = P.op("sync", lambda e: e.dma_start(out=grow[0:1, 0:1], in_=gam_d[:, :]),
              inc=B.sLD, inc_n=16)
    lv = P.op("sync", lambda e: e.dma_start(out=gcols[:, 5:6], in_=valid_d[:, :]),
              inc=B.sLD, inc_n=16)
    lmc = P.op("sync", lambda e: e.dma_start(out=mcl[:], in_=mcl_d[:, :]), inc=B.sLD, inc_n=16)
    lmk = P.op("sync", lambda e: e.dma_start(out=msk[:], in_=msk_d[:, :]), inc=B.sLD, inc_n=16)
    lcp = P.op("sync", lambda e: e.dma_start(out=growp[0:1, 0:128], in_=cpos_d[:, :]),
               inc=B.sLD, inc_n=16)
    lcr = P.op("sync", lambda e: e.dma_start(out=growp[0:1, 128:256], in_=crev_d[:, :]),
               inc=B.sLD, inc_n=16)
    a_ln = P.op("scalar", lambda e: e.activation(grow[0:1, 1:2], grow[0:1, 0:1], AF.Ln),
                waits=[(B.sLD, lg)], inc=B.sA)
    vbf = P.op("vector", lambda e: e.tensor_copy(g_bf[0:1, 0:1], grow[0:1, 1:2]),
               waits=[(B.sA, a_ln)], inc=B.sV)
    m1 = P.op("tensor", lambda e: e.matmul(ps.N[:, 0:1], B.ones_bf[0:1, :], g_bf[0:1, 0:1],
                                           start=True, stop=True),
              waits=[(B.sV, vbf)], inc=B.sMM)
    v_lngc = P.op("vector", lambda e: e.tensor_copy(gcols[:, 0:1], ps.N[:, 0:1]),
                  waits=[(B.sMM, m1)], inc=B.sV)
    a_at = P.op("scalar", lambda e: e.activation(A_T[:], mcl[:], AF.Exp, scale=gcols[:, 0:1]),
                waits=[(B.sV, v_lngc), (B.sLD, lmc)], inc=B.sA)
    v_at = P.op("vector", lambda e: e.tensor_tensor(A_T[:], A_T[:], msk[:], OP.mult),
                waits=[(B.sA, a_at), (B.sLD, lmk)], inc=B.sV)
    a_dr = P.op("scalar", lambda e: e.activation(growp[0:1, 0:128], growp[0:1, 0:128],
                                                 AF.Exp, scale=grow[0:1, 1:2]),
                waits=[(B.sA, a_ln), (B.sLD, lcp)], inc=B.sA)
    a_dv = P.op("scalar", lambda e: e.activation(growp[0:1, 128:256], growp[0:1, 128:256],
                                                 AF.Exp, scale=grow[0:1, 1:2]),
                waits=[(B.sA, a_dr), (B.sLD, lcr)], inc=B.sA)
    a_gc = P.op("scalar", lambda e: e.activation(grow[0:1, 2:3], grow[0:1, 1:2], AF.Exp,
                                                 scale=128.0),
                waits=[(B.sA, a_dv)], inc=B.sA)
    vbf2 = P.op("vector", lambda e: e.tensor_copy(g_bf[0:1, 1:2], grow[0:1, 2:3]),
                waits=[(B.sA, a_gc)], inc=B.sV)
    m2 = P.op("tensor", lambda e: e.matmul(ps.N[:, 1:2], B.ones_bf[0:1, :], g_bf[0:1, 1:2],
                                           start=True, stop=True),
              waits=[(B.sV, vbf2)], inc=B.sMM)
    v_gc = P.op("vector", lambda e: e.tensor_copy(gcols[:, 1:2], ps.N[:, 1:2]),
                waits=[(B.sMM, m2)], inc=B.sV)
    mt1 = P.op("tensor", lambda e: e.transpose(ps.T[:, 0:1], growp[0:1, 0:128],
                                               B.identity_f32[0:1, 0:1]),
               waits=[(B.sA, a_dv), (B.sV, P.now(B.sV))], inc=B.sMM)
    v_dc = P.op("vector", lambda e: e.tensor_copy(gcols[:, 2:3], ps.T[:, 0:1]),
                waits=[(B.sMM, mt1)], inc=B.sV)
    mt2 = P.op("tensor", lambda e: e.transpose(ps.T[:, 1:2], growp[0:1, 128:256],
                                               B.identity_f32[0:1, 0:1]),
               waits=[(B.sV, v_dc)], inc=B.sMM)
    v_dv = P.op("vector", lambda e: e.tensor_copy(gcols[:, 3:4], ps.T[:, 1:2]),
                waits=[(B.sMM, mt2)], inc=B.sV)
    v_dvv = P.op("vector", lambda e: e.tensor_tensor(gcols[:, 4:5], gcols[:, 3:4],
                                                     gcols[:, 5:6], OP.mult),
                 waits=[(B.sV, v_dv), (B.sLD, lv)], inc=B.sV)
    vz1 = P.op("vector", lambda e: e.memset(S_sb[:], 0.0), inc=B.sV)
    vz2 = P.op("vector", lambda e: e.memset(z_sb[:], 0.0), inc=B.sV)
    prep_v = P.now(B.sV)

    def load_w(name):
        return P.op("gpsimd", lambda e: e.dma_start(
            out=Wb.rearrange("p (i n) -> p i n", i=8),
            in_=wts[name].rearrange("(i p) n -> p i n", p=128)),
            waits=[(B.sMM, P.now(B.sMM))], inc=B.sGD, inc_n=16)

    def load_bcol(name):
        return P.op("gpsimd", lambda e: e.dma_start(
            out=bcol.rearrange("p (o j) -> p o j", o=1),
            in_=bias[name].rearrange("o (j p) -> p o j", p=128),
            allow_slow_non_contiguous=True),
            waits=[(B.sA, P.now(B.sA))], inc=B.sGD, inc_n=16)

    def load_brow(name):
        return P.op("gpsimd", lambda e: e.dma_start(out=brow[0:1, 0:D], in_=bias[name][:, :]),
                    waits=[(B.sMM, P.now(B.sMM))], inc=B.sGD, inc_n=16)

    def run_pass(src_ap, own):
        dv_col = gcols[:, 3:4] if own else gcols[:, 4:5]
        xg_guard = [[]]
        for g in range(4):
            xt_v = 0
            for t in range(4):
                gt = g * 4 + t
                ld = P.op("sync", lambda e, gt=gt: e.dma_start(
                    out=xg[:, 0:D], in_=src_ap[gt * 128:(gt + 1) * 128, :]),
                    waits=xg_guard[0], inc=B.sLD, inc_n=16)
                aw, vL = emit_ln(P, B, xg[:, 0:D], xn_bf[:, 0:D], [(B.sLD, ld)])
                for i in range(8):
                    xt_v = emit_transpose(P, B, xn_bf[:, i * 128:(i + 1) * 128],
                                          XTg[:, i * 512 + t * 128:i * 512 + (t + 1) * 128],
                                          tr_slot(ps, t * 8 + i), waits=[aw])
                xg_guard[0] = [(B.sV, xt_v), (B.sA, P.now(B.sA))]
            wk_l = load_w("wkT")
            bk_l = load_bcol("bk")
            for j in range(8):
                bank = [ps.A, ps.B][j % 2]
                mf = None
                for i in range(8):
                    w = ([(B.sV, xt_v), (B.sGD, max(wk_l, bk_l)), (B.sV, P.now(B.sV)),
                          (B.sA, P.now(B.sA))] if i == 0 else [])
                    mf = P.op("tensor", lambda e, i=i, j=j, bank=bank: e.matmul(
                        bank[:, 0:512], Wb[:, i * D + j * 128:i * D + (j + 1) * 128],
                        XTg[:, i * 512:(i + 1) * 512], start=(i == 0), stop=(i == 7)),
                        waits=w, inc=B.sMM)
                P.op("scalar", lambda e, j=j, bank=bank: e.activation(
                    KT_g[:, j * 512:(j + 1) * 512], bank[:, 0:512], AF.Relu,
                    bias=bcol[:, j:j + 1]),
                    waits=[(B.sMM, mf), (B.sGD, bk_l)], inc=B.sA)
            if own:
                wq_l = load_w("wqT")
                bq_l = load_bcol("bq")
                for j in range(8):
                    bank = [ps.A, ps.B][j % 2]
                    mf = None
                    for i in range(8):
                        w = ([(B.sGD, max(wq_l, bq_l)), (B.sV, P.now(B.sV)),
                              (B.sA, P.now(B.sA))] if i == 0 else [])
                        mf = P.op("tensor", lambda e, i=i, j=j, bank=bank: e.matmul(
                            bank[:, 0:512], Wb[:, i * D + j * 128:i * D + (j + 1) * 128],
                            XTg[:, i * 512:(i + 1) * 512], start=(i == 0), stop=(i == 7)),
                            waits=w, inc=B.sMM)
                    P.op("scalar", lambda e, j=j, bank=bank: e.activation(
                        QT_g[:, j * 512:(j + 1) * 512], bank[:, 0:512], AF.Relu,
                        bias=bcol[:, j:j + 1]),
                        waits=[(B.sMM, mf), (B.sGD, bq_l)], inc=B.sA)
            wv_l = load_w("wvT")
            bv_l = load_brow("bv")
            for rt in range(4):
                bank = [ps.A, ps.B][rt % 2]
                for ncol in range(2):
                    mf = None
                    for i in range(8):
                        w = ([(B.sGD, max(wv_l, bv_l)), (B.sV, P.now(B.sV)),
                              (B.sA, P.now(B.sA))] if i == 0 else [])
                        mf = P.op("tensor", lambda e, i=i, rt=rt, ncol=ncol, bank=bank: e.matmul(
                            bank[:, 0:512],
                            XTg[:, i * 512 + rt * 128:i * 512 + (rt + 1) * 128],
                            Wb[:, i * D + ncol * 512:i * D + (ncol + 1) * 512],
                            start=(i == 0), stop=False),
                            waits=w, inc=B.sMM)
                    mf = P.op("tensor", lambda e, ncol=ncol, bank=bank: e.matmul(
                        bank[:, 0:512], B.ones_bf[0:1, :],
                        brow[0:1, ncol * 512:(ncol + 1) * 512],
                        start=False, stop=True),
                        waits=[(B.sGD, bv_l)], inc=B.sMM)
                    P.op("vector", lambda e, rt=rt, ncol=ncol, bank=bank: e.tensor_copy(
                        V_g[:, rt * D + ncol * 512:rt * D + (ncol + 1) * 512], bank[:, 0:512]),
                        waits=[(B.sMM, mf)], inc=B.sV)
            prod_v = P.now(B.sV)
            prod_a = P.now(B.sA)
            for c in range(4):
                gt = g * 4 + c
                kv = 0
                for i in range(8):
                    kv = emit_transpose(P, B, KT_g[:, i * 512 + c * 128:i * 512 + (c + 1) * 128],
                                        krow[:, i * 128:(i + 1) * 128], tr_slot(ps, i),
                                        waits=[(B.sA, prod_a), (B.sV, P.now(B.sV))])
                kdv = P.op("vector", lambda e: e.tensor_scalar(kd_bf[:, 0:D], krow[:, 0:D],
                                                               dv_col, None, OP.mult),
                           waits=[(B.sV, kv), (B.sV, prep_v)], inc=B.sV)
                if own:
                    cs = P.op("vector", lambda e: e.tensor_copy(S_bf[:], S_sb[:]),
                              waits=[(B.sV, P.now(B.sV)), (B.sMM, P.now(B.sMM))], inc=B.sV)
                    cz = P.op("vector", lambda e: e.tensor_copy(z_bf[:], z_sb[:]),
                              waits=[(B.sV, cs)], inc=B.sV)
                    mfp = None
                    for i in range(8):
                        w = [(B.sV, prod_v), (B.sA, prod_a), (B.sV, P.now(B.sV))] if i == 0 else []
                        mfp = P.op("tensor", lambda e, i=i, c=c: e.matmul(
                            ps.T[:, 128:256],
                            KT_g[:, i * 512 + c * 128:i * 512 + (c + 1) * 128],
                            QT_g[:, i * 512 + c * 128:i * 512 + (c + 1) * 128],
                            start=(i == 0), stop=(i == 7)),
                            waits=w, inc=B.sMM)
                    vpp = P.op("vector", lambda e: e.tensor_tensor(pp_bf[:], ps.T[:, 128:256],
                                                                  A_T[:], OP.mult),
                               waits=[(B.sMM, mfp), (B.sV, P.now(B.sV))], inc=B.sV)
                    for dv in range(2):
                        mi = P.op("tensor", lambda e, dv=dv, c=c: e.matmul(
                            ps.C[dv][:, 0:512], pp_bf[:],
                            V_g[:, c * D + dv * 512:c * D + (dv + 1) * 512],
                            start=True, stop=True),
                            waits=[(B.sV, vpp), (B.sV, P.now(B.sV))] if dv == 0 else [],
                            inc=B.sMM)
                        mc = None
                        for i in range(8):
                            mc = P.op("tensor", lambda e, i=i, c=c, dv=dv: e.matmul(
                                ps.C[2 + dv][:, 0:512],
                                QT_g[:, i * 512 + c * 128:i * 512 + (c + 1) * 128],
                                S_bf[:, i * D + dv * 512:i * D + (dv + 1) * 512],
                                start=(i == 0), stop=(i == 7)),
                                waits=([(B.sV, cz)] if (i == 0 and dv == 0) else []),
                                inc=B.sMM)
                    mdi = P.op("tensor", lambda e: e.matmul(
                        ps.N[:, 2:3], pp_bf[:], B.ones_bf[:, 0:1], start=True, stop=True),
                        inc=B.sMM)
                    mdc = None
                    for i in range(8):
                        mdc = P.op("tensor", lambda e, i=i, c=c: e.matmul(
                            ps.N[:, 3:4],
                            QT_g[:, i * 512 + c * 128:i * 512 + (c + 1) * 128],
                            z_bf[:, i:i + 1], start=(i == 0), stop=(i == 7)),
                            inc=B.sMM)
                    num_mm = P.now(B.sMM)
                    e1 = P.op("vector", lambda e: e.tensor_scalar(
                        cross_scr[:, 0:512], ps.C[2][:, 0:512], gcols[:, 2:3], None, OP.mult),
                        waits=[(B.sMM, num_mm)], inc=B.sV)
                    e2 = P.op("vector", lambda e: e.tensor_scalar(
                        cross_scr[:, 512:1024], ps.C[3][:, 0:512], gcols[:, 2:3], None, OP.mult),
                        waits=[(B.sV, e1)], inc=B.sV)
                    d1 = P.op("vector", lambda e: e.tensor_scalar(
                        B.stats[:, 0:1], ps.N[:, 3:4], gcols[:, 2:3], None, OP.mult),
                        waits=[(B.sV, e2)], inc=B.sV)
                    d2 = P.op("vector", lambda e: e.tensor_tensor(
                        B.stats[:, 1:2], ps.N[:, 2:3], B.stats[:, 0:1], OP.add),
                        waits=[(B.sV, d1)], inc=B.sV)
                    d3 = P.op("vector", lambda e: e.tensor_scalar(
                        B.stats[:, 1:2], B.stats[:, 1:2], 1e-6, None, OP.add),
                        waits=[(B.sV, d2)], inc=B.sV)
                    d4 = P.op("vector", lambda e: e.reciprocal(B.stats[:, 2:3], B.stats[:, 1:2]),
                              waits=[(B.sV, d3)], inc=B.sV)
                    xl = P.op("sync", lambda e, gt=gt: e.dma_start(
                        out=xrow[:, 0:D], in_=x_own[gt * 128:(gt + 1) * 128, :]),
                        waits=[(B.sGD, P.now(B.sGD)), (B.sV, P.now(B.sV))], inc=B.sLD, inc_n=16)
                    for dv in range(2):
                        s1 = P.op("vector", lambda e, dv=dv: e.tensor_tensor(
                            scr2[:, dv * 512:(dv + 1) * 512], ps.C[dv][:, 0:512],
                            cross_scr[:, dv * 512:(dv + 1) * 512], OP.add),
                            waits=[(B.sV, d4)], inc=B.sV)
                        s2 = P.op("vector", lambda e, dv=dv: e.tensor_scalar(
                            scr2[:, dv * 512:(dv + 1) * 512], scr2[:, dv * 512:(dv + 1) * 512],
                            B.stats[:, 2:3], None, OP.mult),
                            waits=[(B.sV, s1)], inc=B.sV)
                        s3 = P.op("vector", lambda e, dv=dv: e.tensor_tensor(
                            xrow[:, dv * 512:(dv + 1) * 512], xrow[:, dv * 512:(dv + 1) * 512],
                            scr2[:, dv * 512:(dv + 1) * 512], OP.add),
                            waits=[(B.sV, s2), (B.sLD, xl)], inc=B.sV)
                    P.op("gpsimd", lambda e, gt=gt: e.dma_start(
                        out=x_out[gt * 128:(gt + 1) * 128, :], in_=xrow[:, 0:D]),
                        waits=[(B.sV, P.now(B.sV))], inc=B.sGD, inc_n=16)
                    if RETNET_DEBUG[0] and gt == 0:
                        dpp = nc.dram_tensor("dbg_pp", [128, 128], F32, kind="ExternalOutput").ap()
                        dnum = nc.dram_tensor("dbg_num", [128, D], F32, kind="ExternalOutput").ap()
                        dden = nc.dram_tensor("dbg_den", [128, 8], F32, kind="ExternalOutput").ap()
                        vq1 = P.op("vector", lambda e: e.tensor_copy(B.scr_ln[:, 0:128], pp_bf[:]),
                                   inc=B.sV)
                        P.op("gpsimd", lambda e: e.dma_start(out=dpp[:, :], in_=B.scr_ln[:, 0:128]),
                             waits=[(B.sV, vq1)], inc=B.sGD, inc_n=16)
                        P.op("gpsimd", lambda e: e.dma_start(out=dnum[:, :], in_=scr2[:, 0:D]),
                             inc=B.sGD, inc_n=16)
                        P.op("gpsimd", lambda e: e.dma_start(out=dden[:, :], in_=B.stats[:, 0:8]),
                             inc=B.sGD, inc_n=16)
                st_mm = []
                for dk in range(8):
                    bank = [ps.A, ps.B][dk % 2]
                    for dv in range(2):
                        w = [(B.sV, kdv), (B.sV, P.now(B.sV)), (B.sA, P.now(B.sA))]
                        mm_ = P.op("tensor", lambda e, dk=dk, dv=dv, c=c, bank=bank: e.matmul(
                            bank[:, 0:512],
                            kd_bf[:, dk * 128:(dk + 1) * 128],
                            V_g[:, c * D + dv * 512:c * D + (dv + 1) * 512],
                            start=True, stop=True),
                            waits=w, inc=B.sMM)
                        su = P.op("vector", lambda e, dk=dk, dv=dv, bank=bank: e.scalar_tensor_tensor(
                            S_sb[:, dk * D + dv * 512:dk * D + (dv + 1) * 512],
                            S_sb[:, dk * D + dv * 512:dk * D + (dv + 1) * 512],
                            gcols[:, 1:2], bank[:, 0:512], OP.mult, OP.add),
                            waits=[(B.sMM, mm_)], inc=B.sV)
                        st_mm.append(su)
                    mz = P.op("tensor", lambda e, dk=dk, bank=bank: e.matmul(
                        ps.N[:, 4:5], kd_bf[:, dk * 128:(dk + 1) * 128], B.ones_bf[:, 0:1],
                        start=True, stop=True),
                        waits=[(B.sV, P.now(B.sV))], inc=B.sMM)
                    P.op("vector", lambda e, dk=dk: e.scalar_tensor_tensor(
                        z_sb[:, dk:dk + 1], z_sb[:, dk:dk + 1], gcols[:, 1:2],
                        ps.N[:, 4:5], OP.mult, OP.add),
                        waits=[(B.sMM, mz)], inc=B.sV)

    run_pass(x_prev, False)
    if RETNET_DEBUG[0] and False:
        dA = nc.dram_tensor("dbg_A", [128, 128], F32, kind="ExternalOutput").ap()
        dG = nc.dram_tensor("dbg_g", [128, 8], F32, kind="ExternalOutput").ap()
        dS = nc.dram_tensor("dbg_S", [128, 8 * D], F32, kind="ExternalOutput").ap()
        P.op("gpsimd", lambda e: e.dma_start(out=dA[:, :], in_=A_T[:]),
             waits=[(B.sV, P.now(B.sV)), (B.sMM, P.now(B.sMM))], inc=B.sGD, inc_n=16)
        P.op("gpsimd", lambda e: e.dma_start(out=dG[:, :], in_=gcols[:]),
             inc=B.sGD, inc_n=16)
        P.op("gpsimd", lambda e: e.dma_start(out=dS[:, :], in_=S_sb[:]),
             inc=B.sGD, inc_n=16)
    run_pass(x_own, True)
    if RETNET_DEBUG[0] and False:
        dQ = nc.dram_tensor("dbg_Q", [128, 8 * 512], F32, kind="ExternalOutput").ap()
        dK = nc.dram_tensor("dbg_K", [128, 8 * 512], F32, kind="ExternalOutput").ap()
        dV = nc.dram_tensor("dbg_V", [128, 4 * D], F32, kind="ExternalOutput").ap()
        for nm, (dd, ss, n) in {"q": (dQ, QT_g, 8 * 512), "k": (dK, KT_g, 8 * 512),
                                "v": (dV, V_g, 4 * D)}.items():
            for off in range(0, n, D):
                vv = P.op("vector", lambda e, dd=dd, ss=ss, off=off: e.tensor_copy(
                    scr2[:, 0:D], ss[:, off:off + D]),
                    waits=[(B.sV, P.now(B.sV)), (B.sGD, P.now(B.sGD))], inc=B.sV)
                P.op("gpsimd", lambda e, dd=dd, off=off: e.dma_start(
                    out=dd[:, off:off + D], in_=scr2[:, 0:D]),
                    waits=[(B.sV, vv)], inc=B.sGD, inc_n=16)
    P.op("gpsimd", lambda e: e.nop(), waits=[(B.sGD, P.now(B.sGD))])




_PROG_CACHE = {}


def _get_prog(name, builder):
    if name not in _PROG_CACHE:
        _PROG_CACHE[name] = builder()
    return _PROG_CACHE[name]



def new_sems(P, B, tag):
    for s in ("sLD", "sGD", "sMM", "sV", "sA", "sCC"):
        setattr(B, s, P.sem(s + "_" + tag))


RG_PAIRS = [[0, 1], [2, 3], [4, 5], [6, 7]]


def build_fused(only=None, dbg=False):
    nc, P, B = mkprog()
    ps = alloc_psum(nc)
    declare_consts(nc, P, B, idf32=True)
    alloc_shared(nc, B)
    x_in = nc.dram_tensor("x_in", [R, D], F32, kind="ExternalInput").ap()
    x_fin = nc.dram_tensor("x_fin", [R, D], F32, kind="ExternalOutput").ap()
    x_a = nc.dram_tensor("x_a", [R, D], F32, kind="Internal").ap()
    x_b = nc.dram_tensor("x_b", [R, D], F32, kind="Internal").ap()
    x_ctx = nc.dram_tensor("x_ctxg", [S, D], F32, kind="Internal").ap()
    qt_dram = nc.dram_tensor("qt_scr", [128, 8 * R], BF16, kind="Internal").ap()
    xpp_dram = nc.dram_tensor("xpp_scr", [R, D], F32, kind="Internal").ap()

    P.op("gpsimd", lambda e: e.dma_start(out=x_a[:, :], in_=x_in[:, :]),
         inc=B.sGD, inc_n=16)

    def gather(src):
        prog_barrier(P, B)
        P.op("gpsimd", lambda e: e.collective_compute(
            "AllGather", OP.bypass, ins=[src.opt()], outs=[x_ctx.opt()],
            replica_groups=RG_PAIRS), inc=B.sCC, inc_n=1)
        prog_barrier(P, B)

    cur, nxt = x_a, x_b
    io_b2 = {"pref": "b2_", "qt_dram": qt_dram, "xpp_dram": xpp_dram, "x_ctx": x_ctx}
    io_b6 = {"pref": "b6_"}
    seq = [
        ("b1", emit_L0_attn_a, True, None),
        ("b2", emit_L0_ffn, True, io_b2),
        ("b3", emit_L0_attn_b, True, None),
        ("b4", emit_L0_ffn, True, io_b2),
        ("b5", emit_L1_attn_a, True, None),
        ("b6", emit_L1_ffn, False, io_b6),
        ("b7", emit_L1_attn_b, True, None),
        ("b8", emit_L1_ffn, False, io_b6),
    ]
    if only is not None:
        seq = [s for s in seq if s[0] in only]
    for idx, (tag, emitter, needs_ctx, io_shared) in enumerate(seq):
        prog_barrier(P, B)
        new_sems(P, B, tag)
        if needs_ctx:
            gather(cur)
        else:
            prog_barrier(P, B)
        out_ap = x_fin if idx == len(seq) - 1 else nxt
        if io_shared is not None:
            io = io_shared
            io["first"] = "_dram" not in io
        else:
            io = {"pref": tag + "_", "qt_dram": qt_dram, "xpp_dram": xpp_dram,
                  "x_ctx": x_ctx}
        io["x_own"] = cur
        io["x_out"] = out_ap
        io["x_prev"] = x_ctx[0:R, :]
        emitter(nc, P, B, ps, io)
        if dbg:
            prog_barrier(P, B)
            dbg_ap = nc.dram_tensor("x_dbg_" + tag, [R, D], F32,
                                    kind="ExternalOutput").ap()
            P.op("gpsimd", lambda e, s=out_ap, d=dbg_ap: e.dma_start(
                out=d[:, :], in_=s[:, :]), inc=B.sGD, inc_n=16)
        cur, nxt = out_ap, cur
    prog_barrier(P, B)
    P.op("gpsimd", lambda e: e.nop(),
         waits=[(s, P.now(s)) for s in P.tot])
    P.finalize()
    return nc


def make_common(z):
    t128 = np.arange(128, dtype=np.float32)
    tu = t128[None, :] - t128[:, None]
    C3, S3 = rope_tab_host(S, 50000.0, "half")
    C5, S5 = rope_tab_host(S, 10000.0, "interleave")
    common = dict(
        b1_wqT=np.ascontiguousarray(z["L0_attn_a_Wq"].T),
        b1_wkT=np.ascontiguousarray(z["L0_attn_a_Wk"].T),
        b1_wvT=np.ascontiguousarray(z["L0_attn_a_Wv"].T),
        b1_bq=z["L0_attn_a_bq"].reshape(1, D), b1_bk=z["L0_attn_a_bk"].reshape(1, D),
        b1_bv=z["L0_attn_a_bv"].reshape(1, D),
        b1_gamma=np.asarray(z["L0_attn_a_gamma"], np.float32).reshape(1, 1),
        b1_mclampT=np.maximum(tu, 0).astype(np.float32),
        b1_maskT=(tu >= 0).astype(np.float32),
        b1_cpos=(t128 + 1).reshape(1, 128), b1_crev=(127 - t128).reshape(1, 128),
        b2_wq=z["L0_ffn_Wq"], b2_wk=z["L0_ffn_Wk"], b2_wv=z["L0_ffn_Wv"],
        b2_wo=z["L0_ffn_Wo"], b2_Wg=z["L0_ffn_Wg"], b2_W1=z["L0_ffn_W1"],
        b2_W2=z["L0_ffn_W2"], b2_b1=z["L0_ffn_b1"], b2_b2=z["L0_ffn_b2"],
        b2_ln1_g=z["L0_ffn_ln1_g"].reshape(1, D), b2_ln1_b=z["L0_ffn_ln1_b"].reshape(1, D),
        b2_ln2_g=z["L0_ffn_ln2_g"].reshape(1, D), b2_ln2_b=z["L0_ffn_ln2_b"].reshape(1, D),
        b3_wqT=np.ascontiguousarray(z["L0_attn_b_Wq"].T),
        b3_wkT=np.ascontiguousarray(z["L0_attn_b_Wk"].T),
        b3_wvT=np.ascontiguousarray(z["L0_attn_b_Wv"].T),
        b3_woT=np.ascontiguousarray(z["L0_attn_b_Wo"].T),
        b3_bo=z["L0_attn_b_bo"].reshape(1, D),
        b3_qn_w=z["L0_attn_b_qn_w"].reshape(1, DH), b3_qn_b=z["L0_attn_b_qn_b"].reshape(1, DH),
        b3_kn_w=z["L0_attn_b_kn_w"].reshape(1, DH), b3_kn_b=z["L0_attn_b_kn_b"].reshape(1, DH),
        b3_rc_ctx=C3, b3_rs_ctx=S3,
        b5_wq=z["L1_attn_a_Wq"], b5_wk=z["L1_attn_a_Wk"], b5_wv=z["L1_attn_a_Wv"],
        b5_wo=z["L1_attn_a_Wo"], b5_W1=z["L1_attn_a_W1"], b5_W2=z["L1_attn_a_W2"],
        b5_bq=z["L1_attn_a_bq"].reshape(1, D), b5_bk=z["L1_attn_a_bk"].reshape(1, D),
        b5_bv=z["L1_attn_a_bv"].reshape(1, D), b5_bo=z["L1_attn_a_bo"].reshape(1, D),
        b5_b1=z["L1_attn_a_b1"].reshape(1, 4096), b5_b2=z["L1_attn_a_b2"].reshape(1, D),
        b5_g1=z["L1_attn_a_g1"].reshape(1, D), b5_beta1=z["L1_attn_a_beta1"].reshape(1, D),
        b5_g2=z["L1_attn_a_g2"].reshape(1, D), b5_beta2=z["L1_attn_a_beta2"].reshape(1, D),
        b5_rc_ctx=C5, b5_rs_ctx=S5,
        b6_gate_wT=np.ascontiguousarray(z["L1_ffn_gate_w"].T),
        b6_gate_b=z["L1_ffn_gate_b"].reshape(1, 16),
        b6_w13=z["L1_ffn_w13"], b6_w2=z["L1_ffn_w2"],
        b6_w13_s=z["L1_ffn_w13_s"], b6_w2_s=z["L1_ffn_w2_s"],
        b7_wqT=np.ascontiguousarray(z["L1_attn_b_w_q"].T),
        b7_wkT=np.ascontiguousarray(z["L1_attn_b_w_k"].T),
        b7_wvT=np.ascontiguousarray(z["L1_attn_b_w_v"].T),
        b7_woT=np.ascontiguousarray(z["L1_attn_b_w_out"].T),
        b7_b_q=z["L1_attn_b_b_q"].reshape(1, D), b7_b_k=z["L1_attn_b_b_k"].reshape(1, D),
        b7_b_v=z["L1_attn_b_b_v"].reshape(1, D), b7_b_out=z["L1_attn_b_b_out"].reshape(1, D),
        identity=HOST_CONSTS["identity"], identity_f32=HOST_CONSTS["identity_f32"],
        ones_bf=HOST_CONSTS["ones_bf"],
    )
    return common, C3, S3, C5, S5


def kernel(**inputs):
    z = {k: np.asarray(v) for k, v in inputs.items()}
    nc = _get_prog("fused", build_fused)
    x = np.asarray(z["L0_attn_a_x_seq"], dtype=np.float32)
    common, C3, S3, C5, S5 = make_common(z)
    in_maps = []
    for c in range(NCORES):
        b, half = c // 2, c % 2
        m = dict(common)
        m["x_in"] = np.ascontiguousarray(x[b, half * R:(half + 1) * R])
        m["b1_valid"] = np.full((128, 1), float(half), np.float32)
        m["b3_rc_own"] = np.ascontiguousarray(C3[half * R:(half + 1) * R])
        m["b3_rs_own"] = np.ascontiguousarray(S3[half * R:(half + 1) * R])
        m["b5_rc_own"] = np.ascontiguousarray(C5[half * R:(half + 1) * R])
        m["b5_rs_own"] = np.ascontiguousarray(S5[half * R:(half + 1) * R])
        in_maps.append(m)
    res = run_bass_kernel_spmd(nc, in_maps, core_ids=list(range(NCORES)))
    out = np.empty_like(x)
    for c in range(NCORES):
        b, half = c // 2, c % 2
        out[b, half * R:(half + 1) * R] = res.results[c]["x_fin"]
    return out
